# version 11
# baseline (speedup 1.0000x reference)
import os
import numpy as np
import ml_dtypes
from contextlib import ExitStack

import concourse.bass as bass
import concourse.mybir as mybir
from concourse.bass_utils import run_bass_kernel_spmd

F32 = mybir.dt.float32
BF16 = mybir.dt.bfloat16
AF = mybir.ActivationFunctionType
ALU = mybir.AluOpType
AX = mybir.AxisListType
BFDT = ml_dtypes.bfloat16

NEG = -60000.0

_NAME_CTR = [0]


def uname(base):
    _NAME_CTR[0] += 1
    return f"{base}_u{_NAME_CTR[0]}"


class CFG:
    def __init__(self, S=2048, M=1024):
        self.S = S
        self.M = M
        self.Bc = 2
        self.T = self.Bc * S
        self.NT = self.T // 128
        self.NS = S // 128
        self.D = 1024
        self.DC = 8
        self.E = 8
        self.HD = 4096
        self.H = 16
        self.Dh = 64
        self.Wp = 2 * S
        self.LB = 256
        self.TK = 16
        self.CH = 32
        self.NW = self.LB // 128 + 1


class Prog:
    DMA_QUEUES = ("qs", "qa", "qg")
    NSUB = 4

    def __init__(self, nc, block, sems):
        self.nc, self.block = nc, block
        self.streams = {
            "t": "tensor", "v": "vector", "s": "scalar", "g": "gpsimd",
            "qs": "sync", "qa": "scalar", "qg": "gpsimd",
        }
        for q in self.DMA_QUEUES:
            for i in range(self.NSUB):
                self.streams[(q, i)] = self.streams[q]
        self.sems = sems
        self.qidx = {q: 0 for q in self.DMA_QUEUES}
        self.cnt = {k: 0 for k in self.streams}
        self.watermark = {}
        self.last_w = {}
        self.readers = {}
        self.pend = {s: [] for s in set(self.streams.values())}

    MAX_BLOCK = int(os.environ.get('ATHENA_MAXBLK', '512'))

    PACKED = ("tensor", "vector", "scalar", "sync")

    def _flush_stream(self, stream):
        q = self.pend[stream]
        if q:
            def runner(h, fns=tuple(q)):
                for g in fns:
                    g(h)
            getattr(self.block, stream)(runner)
            self.pend[stream] = []

    def _emit(self, stream, f, is_dma=False):
        self.pend[stream].append(f)
        self._flush_stream(stream)

    def flush(self):
        for stream in sorted(self.pend):
            self._flush_stream(stream)

    def op(self, ek, emit, reads=(), writes=(), inc=None):
        if ek in self.DMA_QUEUES:
            sub = self.qidx[ek]
            self.qidx[ek] = (sub + 1) % self.NSUB
            ek = (ek, sub)
            if inc is None:
                inc = 16
        inc = inc if inc is not None else 1
        stream = self.streams[ek]
        need = {}

        def _add(semkey, val):
            if val > need.get(semkey, 0):
                need[semkey] = val

        for k in reads:
            lw = self.last_w.get(k)
            if lw:
                _add(*lw)
        for k in writes:
            lw = self.last_w.get(k)
            if lw:
                _add(*lw)
            for semkey, val in self.readers.get(k, {}).items():
                _add(semkey, val)
        waits = []
        for semkey in sorted(need, key=str):
            val = need[semkey]
            if semkey == "t" and stream == "tensor":
                continue
            if self.watermark.get((stream, semkey), 0) >= val:
                continue
            waits.append((self.sems[semkey], val))
            self.watermark[(stream, semkey)] = val
        myval = self.cnt[ek] + inc
        sem = self.sems[ek]
        self_wait = self.cnt[ek] if (isinstance(ek, tuple) and self.cnt[ek] > 0) else None

        def f(h):
            if self_wait is not None:
                h.wait_ge(sem, self_wait)
            for s, v in waits:
                h.wait_ge(s, v)
            inst = emit(h)
            inst.then_inc(sem, inc)

        self._emit(stream, f, is_dma=(ek in self.DMA_QUEUES))
        self.cnt[ek] = myval
        for k in writes:
            self.last_w[k] = (ek, myval)
            self.readers[k] = {}
        for k in reads:
            d = self.readers.setdefault(k, {})
            if myval > d.get(ek, 0):
                d[ek] = myval
        return myval

    def dma(self, ek, dst, src, reads=(), writes=(), **kw):
        return self.op(ek, lambda h: h.dma_start(dst, src, **kw),
                       reads=reads, writes=writes)

    def full_barrier(self):
        self.flush()
        snap = {k: self.cnt[k] for k in self.streams}
        for stream in sorted(set(self.streams.values())):
            def f(h, snap=snap):
                seen = {}
                for k, v in snap.items():
                    if v > 0:
                        hnd = self.sems[k]
                        prev = seen.get(id(hnd), (hnd, 0))[1]
                        seen[id(hnd)] = (hnd, max(v, prev))
                for hnd, v in seen.values():
                    h.wait_ge(hnd, v)
            getattr(self.block, stream)(f)
            for k in self.streams:
                self.watermark[(stream, k)] = max(
                    self.watermark.get((stream, k), 0), snap[k])

    def drain(self):
        todo = {}
        for k in self.streams:
            if self.cnt[k] > 0:
                h = self.sems[k]
                todo[id(h)] = (h, max(self.cnt[k], todo.get(id(h), (h, 0))[1]))

        self.flush()

        def f(h):
            for s, v in todo.values():
                h.wait_ge(s, v)
        self.block.sync(f)


class Common:
    pass


def alloc_common(stk, p, nc, cfg, names):
    c = Common()
    c.d = names
    c.ident_f = stk.enter_context(nc.sbuf_tensor([128, 128], F32))
    c.ident_b = stk.enter_context(nc.sbuf_tensor([128, 128], BF16))
    p.dma("qs", c.ident_f[:], names["ident_f"], writes=[("c", "idf")])
    p.dma("qs", c.ident_b[:], names["ident_b"], writes=[("c", "idb")])
    c.consts = stk.enter_context(nc.sbuf_tensor([128, 4], F32))
    p.op("v", lambda v: v.memset(c.consts[:, 0:1], 1e-5), writes=[("c", "k")])
    p.op("v", lambda v: v.memset(c.consts[:, 1:2], 1e-12), writes=[("c", "k")])
    c.sc_f32 = stk.enter_context(nc.sbuf_tensor([128, 1024], F32))
    c.st_small = stk.enter_context(nc.sbuf_tensor([128, 20], F32))
    c.x_sb = [stk.enter_context(nc.sbuf_tensor(uname(f"xsb{i}"), [128, 1024], F32)) for i in range(2)]
    c.xn_sb = c.x_sb
    return c


def emit_ln(p, c, slot, xkey, epscol=0):
    x = c.x_sb[slot]
    st = c.st_small
    kst = ("lnst",)
    p.op("v", lambda v: v.bn_stats(st[:, 0:6], x[:, 0:512]), reads=[xkey], writes=[kst])
    p.op("v", lambda v: v.bn_stats(st[:, 6:12], x[:, 512:1024]), reads=[xkey], writes=[kst])
    p.op("v", lambda v: v.bn_aggr(st[:, 12:14], st[:, 0:12]), reads=[kst], writes=[kst])
    p.op("s", lambda s: s.activation(st[:, 14:15], st[:, 13:14], AF.Ln,
                                     bias=c.consts[:, epscol:epscol + 1], scale=1.0),
         reads=[kst, ("c", "k")], writes=[kst])
    p.op("s", lambda s: s.activation(st[:, 15:16], st[:, 14:15], AF.Exp,
                                     bias=0.0, scale=-0.5),
         reads=[kst], writes=[kst])
    p.op("v", lambda v: v.tensor_scalar(st[:, 16:17], st[:, 12:13], -1.0, None, ALU.mult),
         reads=[kst], writes=[kst])
    p.op("v", lambda v: v.tensor_tensor(st[:, 17:18], st[:, 16:17], st[:, 15:16], ALU.mult),
         reads=[kst], writes=[kst])
    p.op("s", lambda s: s.activation(c.sc_f32[:], x[:], AF.Identity,
                                     bias=st[:, 17:18], scale=st[:, 15:16]),
         reads=[xkey, kst], writes=[("ysc",)])


def emit_transpose_f32(p, c, src_sb, src_key, ps, dst_emits):
    pkey = ("ps", "tr")

    def f(t):
        last = None
        for cc in range(8):
            last = t.transpose(ps[:, cc * 128:(cc + 1) * 128],
                               src_sb[:, cc * 128:(cc + 1) * 128], c.ident_f[:])
        return last
    p.op("t", f, reads=[src_key, ("c", "idf")], writes=[pkey])
    for ek, emitfn, rds, wrs in dst_emits:
        p.op(ek, emitfn, reads=list(rds) + [pkey], writes=list(wrs))


def emit_phase_l1ffn(p, stk, nc, c, cfg, xin, xout, xin_tag, xout_tag):
    d = c.d
    with ExitStack() as ph:
        win_sb = ph.enter_context(nc.sbuf_tensor([128, 8, 64], BF16))
        w1_sb = ph.enter_context(nc.sbuf_tensor([128, 16], F32))
        b1_sb = ph.enter_context(nc.sbuf_tensor([128, 16], F32))
        w2_sb = ph.enter_context(nc.sbuf_tensor([128, 16], F32))
        bin_sb = ph.enter_context(nc.sbuf_tensor([128, 1], F32))
        b2_sb = ph.enter_context(nc.sbuf_tensor([128, 1], F32))
        wout_sb = ph.enter_context(nc.sbuf_tensor([128, 1024], BF16))
        boutb = ph.enter_context(nc.sbuf_tensor([128, 1024], F32))
        yT = ph.enter_context(nc.sbuf_tensor([128, 8, 128], BF16))
        sT = ph.enter_context(nc.sbuf_tensor([128, 128], F32))
        tmp = ph.enter_context(nc.sbuf_tensor([128, 128], F32))
        gT = ph.enter_context(nc.sbuf_tensor([128, 128], F32))
        gB = ph.enter_context(nc.sbuf_tensor([128, 128], BF16))
        ps_t = ph.enter_context(nc.psum_tensor([128, 1024], F32))
        ps_s = ph.enter_context(nc.psum_tensor([128, 128], F32))
        ps_o = [ph.enter_context(nc.psum_tensor(uname(f"psffo{i}"), [128, 1024], F32)) for i in range(2)]

        p.dma("qs", win_sb[:], d["win"].rearrange("(db p) m -> p db m", p=128),
              writes=[("w", "win")])
        p.dma("qs", w1_sb[:], d["w1s"], writes=[("w", "w1")])
        p.dma("qs", b1_sb[:], d["b1s"], writes=[("w", "b1")])
        p.dma("qs", w2_sb[:], d["w2s"], writes=[("w", "w2")])
        p.dma("qs", bin_sb[:], d["bins"], writes=[("w", "bin")])
        p.dma("qs", b2_sb[:], d["b2s"], writes=[("w", "b2")])
        p.dma("qs", wout_sb[:], d["wouts"], writes=[("w", "wout")])
        p.dma("qs", boutb[:], bass.AP(d["bout"].tensor, 0, [[0, 128], [1, 1024]]),
              writes=[("w", "boutb")])
        p.op("v", lambda v: v.tensor_scalar(boutb[:], boutb[:], 0.5, None, ALU.mult),
             reads=[("w", "boutb")], writes=[("w", "boutb")])

        for pair in range(cfg.NT // 2):
            t0 = 2 * pair
            for j in range(2):
                t = t0 + j
                slot = t % 2
                p.dma("qs", c.x_sb[slot][:], xin[t * 128:(t + 1) * 128, :],
                      reads=[(xin_tag, t)], writes=[("xsb", slot)])
                emit_ln(p, c, slot, ("xsb", slot))

                def cp_bf(v):
                    return v.tensor_copy(
                        yT[:],
                        ps_t[:].rearrange("p (cc n) -> p cc n", cc=8))
                emit_transpose_f32(p, c, c.sc_f32, ("ysc",), ps_t,
                                   [("v", cp_bf, [], [("yT",)])])

                def mk_mm(jj):
                    def g(t_):
                        last = None
                        for db in range(8):
                            last = t_.matmul(
                                ps_s[jj * 64:(jj + 1) * 64, :],
                                win_sb[:, db, :],
                                yT[:, db, :],
                                start=(db == 0), stop=(db == 7),
                                tile_position=(0, jj * 64))
                        return last
                    return g
                p.op("t", mk_mm(j), reads=[("yT",), ("w", "win")],
                     writes=[("ps", "s", j)])
            p.op("v", lambda v: v.tensor_scalar(sT[:], ps_s[:], bin_sb[:],
                                                None, ALU.add),
                 reads=[("ps", "s", 0), ("ps", "s", 1), ("w", "bin")],
                 writes=[("sT",)])
            for i in range(16):
                p.op("v", lambda v, i=i: v.tensor_scalar(
                        tmp[:], sT[:], w1_sb[:, i:i + 1],
                        b1_sb[:, i:i + 1], ALU.mult, ALU.add),
                     reads=[("sT",), ("w", "w1"), ("w", "b1")], writes=[("tmp",)])
                p.op("v", lambda v: v.tensor_single_scalar(
                        tmp[:], tmp[:], 0.0, ALU.max),
                     reads=[("tmp",)], writes=[("tmp",)])
                if i == 0:
                    p.op("v", lambda v, i=i: v.tensor_scalar(
                            gT[:], tmp[:], w2_sb[:, i:i + 1], None, ALU.mult),
                         reads=[("tmp",), ("w", "w2")], writes=[("gT",)])
                else:
                    p.op("v", lambda v, i=i: v.scalar_tensor_tensor(
                            gT[:], tmp[:], w2_sb[:, i:i + 1],
                            gT[:], ALU.mult, ALU.add),
                         reads=[("tmp",), ("w", "w2"), ("gT",)], writes=[("gT",)])
            p.op("v", lambda v: v.tensor_scalar(
                    gB[:], gT[:], b2_sb[:], None, ALU.add),
                 reads=[("gT",), ("w", "b2")], writes=[("gB",)])
            for j in range(2):
                t = t0 + j
                slot = t % 2

                def mk_o(jj):
                    def g(t_):
                        last = None
                        for nch in range(2):
                            last = t_.matmul(
                                ps_o[jj][:, nch * 512:(nch + 1) * 512],
                                gB[jj * 64:(jj + 1) * 64, :],
                                wout_sb[jj * 64:(jj + 1) * 64,
                                        nch * 512:(nch + 1) * 512],
                                start=True, stop=True,
                                tile_position=(jj * 64, 0))
                        return last
                    return g
                p.op("t", mk_o(j), reads=[("gB",), ("w", "wout")],
                     writes=[("ps", "o", j)])
                p.op("v", lambda v, jj=j, sl=slot: v.scalar_tensor_tensor(
                        c.xn_sb[sl][:], ps_o[jj][:], 0.5, c.x_sb[sl][:],
                        ALU.mult, ALU.add),
                     reads=[("ps", "o", j), ("xsb", slot)], writes=[("xsb", slot)])
                p.op("v", lambda v, sl=slot: v.tensor_tensor(
                        c.xn_sb[sl][:], c.xn_sb[sl][:], boutb[:], ALU.add),
                     reads=[("xsb", slot), ("w", "boutb")], writes=[("xsb", slot)])
                p.dma("qa", xout[t * 128:(t + 1) * 128, :], c.xn_sb[slot][:],
                      reads=[("xsb", slot)], writes=[(xout_tag, t)])


def emit_phase_moe(p, stk, nc, c, cfg, xin, xout, xin_tag, xout_tag):
    d = c.d
    T, NT = cfg.T, cfg.NT
    NG = T // 512
    out_d = nc.dram_tensor(uname("moeoutd"), [T, 1024], BF16, kind="Internal").ap()
    with ExitStack() as ph:
        yT = ph.enter_context(nc.sbuf_tensor([128, 8, T], BF16))
        stg = ph.enter_context(nc.sbuf_tensor([128, 4, 2, 512], BF16))
        outt = ph.enter_context(nc.sbuf_tensor([128, 1024], BF16))
        mask_sb = ph.enter_context(nc.sbuf_tensor([128, NT, 8], F32))
        fcw = [ph.enter_context(nc.sbuf_tensor(uname(f"fcw{i}"), [128, 8, 2048], BF16))
               for i in range(2)]
        pjw = ph.enter_context(nc.sbuf_tensor([128, 16, 1024], BF16))
        hm = ph.enter_context(nc.sbuf_tensor([128, 16, 512], BF16))
        rt_sb = [ph.enter_context(nc.sbuf_tensor(uname(f"rtsb{i}"), [128, 512], BF16)) for i in range(2)]
        rwT_sb = ph.enter_context(nc.sbuf_tensor([128, 8, 8], F32))
        p.dma("qs", rwT_sb[:], d["rwT"].rearrange("(db p) e -> p db e", p=128),
              writes=[("w", "rwT")])

        with ExitStack() as pre:
            ytf = pre.enter_context(nc.sbuf_tensor([128, 1024], F32))
            lgt = pre.enter_context(nc.sbuf_tensor([128, 16], F32))
            ps_t = pre.enter_context(nc.psum_tensor([128, 1024], F32))
            ps_l = pre.enter_context(nc.psum_tensor([128, 8], F32))
            for t in range(NT):
                slot = t % 2
                p.dma("qs", c.x_sb[slot][:], xin[t * 128:(t + 1) * 128, :],
                      reads=[(xin_tag, t)], writes=[("xsb", slot)])
                emit_ln(p, c, slot, ("xsb", slot))

                def cp_f(v):
                    return v.tensor_copy(ytf[:], ps_t[:])

                def cp_b(s, tt=t):
                    return s.copy(yT[:, :, tt * 128:(tt + 1) * 128],
                                  ps_t[:].rearrange("p (cc n) -> p cc n", cc=8))
                emit_transpose_f32(p, c, c.sc_f32, ("ysc",), ps_t,
                                   [("v", cp_f, [], [("ytf",)]),
                                    ("s", cp_b, [], [("yT", t)])])

                def mk_lg(t_):
                    last = None
                    for db in range(8):
                        last = t_.matmul(ps_l[:], ytf[:, db * 128:(db + 1) * 128],
                                         rwT_sb[:, db, :],
                                         start=(db == 0), stop=(db == 7))
                    return last
                p.op("t", mk_lg, reads=[("ytf",), ("w", "rwT")], writes=[("ps", "l")])
                p.op("v", lambda v: v.tensor_copy(lgt[:, 0:8], ps_l[:]),
                     reads=[("ps", "l")], writes=[("lgt",)])
                p.op("v", lambda v: v.reduce_max(lgt[:, 8:9], lgt[:, 0:8], axis=AX.X),
                     reads=[("lgt",)], writes=[("lgt",)])
                p.op("v", lambda v, tt=t: v.tensor_scalar(
                        mask_sb[:, tt, :], lgt[:, 0:8], lgt[:, 8:9], None,
                        ALU.is_equal),
                     reads=[("lgt",)], writes=[("mask", t)])
            p.full_barrier()

        with ExitStack() as mn:
            ps_h = [mn.enter_context(nc.psum_tensor(uname(f"psh{i}"), [128, 512], F32)) for i in range(2)]
            ps_o = [mn.enter_context(nc.psum_tensor(uname(f"pso{i}"), [128, 512], F32)) for i in range(4)]
            for e in range(cfg.E):
                for half in range(2):
                    eh = e * 2 + half
                    fs = eh % 2
                    fsrc = d["fc"][e].rearrange("(db p) hc -> p db hc", p=128)[
                        :, :, half * 2048:(half + 1) * 2048]
                    psrc = d["pjT"][e].rearrange("(cc p) dd -> p cc dd", p=128)[
                        :, half * 16:(half + 1) * 16, :]
                    p.dma("qs", fcw[fs][:], fsrc, writes=[("fcw", fs)])
                    p.dma("qs", pjw[:], psrc, writes=[("pjw",)])
                    for g in range(NG):
                        ytkeys = [("yT", 4 * g + i) for i in range(4)]
                        for cc in range(16):
                            hs = cc % 2

                            def mk1(cc=cc, hs=hs, g=g, fs=fs):
                                def f(t_):
                                    last = None
                                    for db in range(8):
                                        last = t_.matmul(
                                            ps_h[hs][:],
                                            fcw[fs][:, db, cc * 128:(cc + 1) * 128],
                                            yT[:, db, g * 512:(g + 1) * 512],
                                            start=(db == 0), stop=(db == 7))
                                    return last
                                return f
                            p.op("t", mk1(), reads=[("fcw", fs)] + ytkeys,
                                 writes=[("ps", "h", hs)])
                            p.op("v", lambda v, hs=hs: v.tensor_single_scalar(
                                    rt_sb[hs][:], ps_h[hs][:], 0.0, ALU.max),
                                 reads=[("ps", "h", hs)], writes=[("rt", hs)])
                            p.op("g", lambda gg, hs=hs, cc=cc: gg.tensor_tensor(
                                    hm[:, cc, :], rt_sb[hs][:], rt_sb[hs][:],
                                    ALU.mult),
                                 reads=[("rt", hs)], writes=[("hm", cc)])
                        hmkeys = [("hm", cc2) for cc2 in range(16)]
                        for dh in range(2):
                            for lt in range(4):
                                def mk2(dh=dh, lt=lt):
                                    def f(t_):
                                        last = None
                                        for cc in range(16):
                                            last = t_.matmul(
                                                ps_o[lt][:],
                                                hm[:, cc, lt * 128:(lt + 1) * 128],
                                                pjw[:, cc, dh * 512:(dh + 1) * 512],
                                                start=(cc == 0), stop=(cc == 15))
                                        return last
                                    return f
                                p.op("t", mk2(), reads=hmkeys + [("pjw",)],
                                     writes=[("ps", "o", lt)])
                            for lt in range(4):
                                t = g * 4 + lt
                                p.op("v", lambda v, lt=lt, dh=dh, t=t, e=e:
                                     v.tensor_scalar(
                                        stg[:, lt, dh, :],
                                        ps_o[lt][:], mask_sb[:, t, e:e + 1],
                                        None, ALU.mult),
                                     reads=[("ps", "o", lt), ("mask", t)],
                                     writes=[("stg",)])
                        dst = out_d[g * 512:(g + 1) * 512, :].rearrange(
                            "(lt pp) (dh cc) -> pp lt dh cc", pp=128, dh=2)
                        kw = {} if (e == 0 and half == 0) else {"accum_op": ALU.add}
                        p.dma("qg", dst, stg[:],
                              reads=[("stg",)], writes=[("outd", g)], **kw)
            p.full_barrier()

        for t in range(NT):
            slot = t % 2
            p.dma("qs", c.x_sb[slot][:], xin[t * 128:(t + 1) * 128, :],
                  reads=[(xin_tag, t)], writes=[("xsb", slot)])
            p.dma("qs", outt[:], out_d[t * 128:(t + 1) * 128, :],
                  reads=[("outd", t // 4)], writes=[("outt",)])
            p.op("v", lambda v, t=t, sl=slot: v.scalar_tensor_tensor(
                    c.xn_sb[sl][:], outt[:], 0.5, c.x_sb[sl][:],
                    ALU.mult, ALU.add),
                 reads=[("outt",), ("xsb", slot)], writes=[("xsb", slot)])
            p.dma("qa", xout[t * 128:(t + 1) * 128, :], c.xn_sb[slot][:],
                  reads=[("xsb", slot)], writes=[(xout_tag, t)])


def emit_phase_l0attn(p, stk, nc, c, cfg, xin, xout, xin_tag, xout_tag):
    d = c.d
    S, T, NT, NS, Wp = cfg.S, cfg.T, cfg.NT, cfg.NS, cfg.Wp
    NRT = S // 128
    NLQ = S // 512
    NJ = Wp // 512
    qdr_d = [nc.dram_tensor(uname(f"qdrd{i}"), [S, Wp], BF16, kind="Internal").ap()
             for i in range(4)]
    kdx_d = [nc.dram_tensor(uname(f"kdxd{i}"), [S, Wp], BF16, kind="Internal").ap()
             for i in range(4)]
    ctx_d = nc.dram_tensor("ctxd", [cfg.Bc, 8, 128, S], BF16, kind="Internal").ap()
    ytmp_d = nc.dram_tensor("ytmpd", [T, 1024], BF16, kind="Internal").ap()
    zrow_d = nc.dram_tensor("zrowd", [1, S], F32, kind="Internal").ap()

    with ExitStack() as ph:
        yT = ph.enter_context(nc.sbuf_tensor([128, 8, T], BF16))
        ybf = ph.enter_context(nc.sbuf_tensor([128, 1024], BF16))
        with ExitStack() as pre:
            ps_t = pre.enter_context(nc.psum_tensor([128, 1024], F32))
            for t in range(NT):
                slot = t % 2
                p.dma("qs", c.x_sb[slot][:], xin[t * 128:(t + 1) * 128, :],
                      reads=[(xin_tag, t)], writes=[("xsb", slot)])
                emit_ln(p, c, slot, ("xsb", slot))
                p.op("v", lambda v: v.tensor_copy(ybf[:], c.sc_f32[:]),
                     reads=[("ysc",)], writes=[("ybf",)])
                p.dma("qa", ytmp_d[t * 128:(t + 1) * 128, :], ybf[:],
                      reads=[("ybf",)], writes=[("ytmp", t)])

                def cp_b(s, tt=t):
                    return s.copy(yT[:, :, tt * 128:(tt + 1) * 128],
                                  ps_t[:].rearrange("p (cc n) -> p cc n", cc=8))
                emit_transpose_f32(p, c, c.sc_f32, ("ysc",), ps_t,
                                   [("s", cp_b, [], [("yT", t)])])
            p.full_barrier()
        ytk = [("yT", t) for t in range(NT)]

        with ExitStack() as mn:
            ps_big = mn.enter_context(nc.psum_tensor([128, 2048], F32))
            ps_st = [mn.enter_context(nc.psum_tensor(uname(f"psst{i}"), [128, 1024], F32))
                     for i in range(2)]
            ps_u = ps_st[0]
            ps_bc = ps_st[1]
            pbq = [("psbig", q4) for q4 in range(4)]
            ex_t = mn.enter_context(nc.sbuf_tensor([128, Wp], BF16))
            exr_t = mn.enter_context(nc.sbuf_tensor([128, Wp], BF16))
            p.dma("qs", ex_t[:], d["extk"], writes=[("ext",)])
            p.dma("qs", exr_t[:], d["exrtk"], writes=[("exrt",)])
            wsl = [{nm: mn.enter_context(nc.sbuf_tensor(uname(f"wsl{i}_{nm}"),
                                                        [128, 8, 128], BF16))
                    for nm in ("q", "k", "v")} for i in range(2)]
            bqp_sb = mn.enter_context(nc.sbuf_tensor([128, 8], F32))
            bkp_sb = mn.enter_context(nc.sbuf_tensor([128, 8], F32))
            bvp_sb = mn.enter_context(nc.sbuf_tensor([128, 8], F32))
            p.dma("qs", bqp_sb[:], d["bqp"], writes=[("w", "bqp")])
            p.dma("qs", bkp_sb[:], d["bkp"], writes=[("w", "bkp")])
            p.dma("qs", bvp_sb[:], d["bvp"], writes=[("w", "bvp")])
            qT_p = [mn.enter_context(nc.sbuf_tensor(uname(f"qTp{i}"), [128, S], BF16))
                    for i in range(2)]
            kT_p = [mn.enter_context(nc.sbuf_tensor(uname(f"kTp{i}"), [128, S], BF16))
                    for i in range(2)]
            vau = [mn.enter_context(nc.sbuf_tensor(uname(f"vau{i}"), [128, 2, NRT, 65],
                                                   BF16)) for i in range(2)]
            rel = [mn.enter_context(nc.sbuf_tensor(uname(f"rel{i}"), [128, S], BF16))
                   for i in range(3)]
            qdsb = [mn.enter_context(nc.sbuf_tensor(uname(f"qdsb{i}"), [128, Wp], BF16))
                    for i in range(2)]
            squt = mn.enter_context(nc.sbuf_tensor([128, 512], BF16))
            stile = mn.enter_context(nc.sbuf_tensor([128, 2048], F32))
            ptile = mn.enter_context(nc.sbuf_tensor([128, 2048], BF16))
            ones_c = mn.enter_context(nc.sbuf_tensor([128, 1], BF16))
            ones_r = mn.enter_context(nc.sbuf_tensor([1, 128], BF16))
            usml = mn.enter_context(nc.sbuf_tensor([128, 16], F32))
            usmlb = mn.enter_context(nc.sbuf_tensor([128, 2], BF16))
            negu = mn.enter_context(nc.sbuf_tensor([128, 1], F32))
            zraw = mn.enter_context(nc.sbuf_tensor([128, S], F32))
            z0 = mn.enter_context(nc.sbuf_tensor([1, S], F32))
            z0b = mn.enter_context(nc.sbuf_tensor([1, S], BF16))
            ctxsb = mn.enter_context(nc.sbuf_tensor([64, S], BF16))
            ctxn = mn.enter_context(nc.sbuf_tensor([64, 512], BF16))
            p.op("v", lambda v: v.memset(ones_c[:], 1.0), writes=[("onec",)])
            p.dma("qs", ones_r[:], d["ones_row"], writes=[("oner",)])
            for i in range(2):
                p.op("v", lambda v, i=i: v.memset(vau[i][:], 0.0),
                     writes=[("vau", i, 0), ("vau", i, 1)])


            kex = ("emax",)
            p.op("v", lambda v: v.memset(usml[0:1, 0:1], 0.0), writes=[kex])
            for jc in range(NJ):
                p.op("g", lambda g, jc=jc: g.tensor_tensor(
                        squt[:], ex_t[:, jc * 512:(jc + 1) * 512],
                        ex_t[:, jc * 512:(jc + 1) * 512], ALU.mult),
                     reads=[("ext",)], writes=[("squt",)])
                p.op("t", lambda t_: t_.matmul(
                        ps_u[0:1, 0:512], ones_c[0:64, :], squt[0:64, :],
                        start=True, stop=True),
                     reads=[("squt",), ("onec",)], writes=[("ps", "st", 0)])
                p.op("v", lambda v: v.reduce_max(usml[0:1, 1:2], ps_u[0:1, 0:512],
                                                 axis=AX.X),
                     reads=[("ps", "st", 0)], writes=[("um1",)])
                p.op("v", lambda v: v.tensor_tensor(
                        usml[0:1, 0:1], usml[0:1, 0:1], usml[0:1, 1:2], ALU.max),
                     reads=[("um1",), kex], writes=[kex])
            p.op("s", lambda s: s.activation(usml[0:1, 3:4], usml[0:1, 0:1],
                                             AF.Ln),
                 reads=[kex], writes=[kex])
            p.op("s", lambda s: s.activation(usml[0:1, 2:3], usml[0:1, 3:4],
                                             AF.Exp, bias=0.0, scale=0.5),
                 reads=[kex], writes=[kex])

            def s1_proj(i):
                hp, b = divmod(i, cfg.Bc)
                par = i % 2
                ws = hp % 2
                if b == 0:
                    for nm, wname in (("q", "wq"), ("k", "wk"), ("v", "wv")):
                        srcw = d[wname][:, hp * 128:(hp + 1) * 128].rearrange(
                            "(eb p) dc -> p eb dc", p=128)
                        p.dma("qs", wsl[ws][nm][:], srcw, writes=[("wsl", ws, nm)])
                ts = b * S
                for nm, outbuf, okey, bkey, bias_sb in (
                        ("q", qT_p[par], ("qTp", par), ("w", "bqp"), bqp_sb),
                        ("k", kT_p[par], ("kTp", par), ("w", "bkp"), bkp_sb)):
                    for nchs in range(S // 512):
                        pp = nchs % 2

                        def mkp(nm=nm, nchs=nchs, pp=pp, ts=ts, ws=ws):
                            def f(t_):
                                last = None
                                for eb in range(8):
                                    last = t_.matmul(
                                        ps_st[pp][:, 0:512],
                                        wsl[ws][nm][:, eb, :],
                                        yT[:, eb, ts + nchs * 512:
                                           ts + (nchs + 1) * 512],
                                        start=(eb == 0), stop=(eb == 7))
                                return last
                            return f
                        p.op("t", mkp(), reads=[("wsl", ws, nm)] + ytk,
                             writes=[("ps", "st", pp)])
                        p.op("v", lambda v, outbuf=outbuf, nchs=nchs, pp=pp,
                             bias_sb=bias_sb, hp=hp: v.tensor_scalar(
                                outbuf[:, nchs * 512:(nchs + 1) * 512],
                                ps_st[pp][:, 0:512],
                                bias_sb[:, hp:hp + 1], None, ALU.add),
                             reads=[("ps", "st", pp), bkey],
                             writes=[okey])
                for lt in range(NRT):
                    pp = lt % 2

                    def mkv(lt=lt, pp=pp, ts=ts, ws=ws):
                        def f(t_):
                            last = None
                            for eb in range(8):
                                last = t_.matmul(
                                    ps_st[pp][:, 0:128],
                                    yT[:, eb, ts + lt * 128:ts + (lt + 1) * 128],
                                    wsl[ws]["v"][:, eb, :],
                                    start=(eb == 0), stop=(eb == 7))
                            return last
                        return f
                    p.op("t", mkv(), reads=[("wsl", ws, "v")] + ytk,
                         writes=[("ps", "st", pp)])
                    for h2 in range(2):
                        p.op("v", lambda v, lt=lt, pp=pp, h2=h2, par=par:
                             v.tensor_copy(
                                vau[par][:, h2, lt, 0:64],
                                ps_st[pp][0:128, h2 * 64:(h2 + 1) * 64]),
                             reads=[("ps", "st", pp)], writes=[("vau", par, h2)])
                for h2 in range(2):
                    p.op("v", lambda v, h2=h2, par=par: v.memset(
                            vau[par][:, h2, :, 64:65], 1.0),
                         writes=[("vau", par, h2)])

            def s2_tables(i):
                par = i % 2
                for h2 in range(2):
                    tslot = 2 * par + h2
                    for tabname, srcT, srckey, srcbuf, sbkey, dstd, dkey, qq in (
                            ("qdr", exr_t, ("exrt",), qT_p[par], ("qTp", par),
                             qdr_d, "qd", "qg"),
                            ("kdx", ex_t, ("ext",), kT_p[par], ("kTp", par),
                             kdx_d, "kd", "qs")):
                        for lt in range(NRT):
                            nquad = (NJ + 1) // 2
                            for quad in range(nquad):
                                njh = min(2, NJ - quad * 2)
                                qb = (quad % 2) * 2

                                def mkt(h2=h2, lt=lt, quad=quad, njh=njh,
                                        qb=qb, srcbuf=srcbuf, srcT=srcT):
                                    def f(t_):
                                        last = None
                                        for jj in range(njh):
                                            jc = quad * 2 + jj
                                            last = t_.matmul(
                                                ps_big[:, (qb + jj) * 512:
                                                       (qb + jj + 1) * 512],
                                                srcbuf[h2 * 64:(h2 + 1) * 64,
                                                       lt * 128:(lt + 1) * 128],
                                                srcT[h2 * 64:(h2 + 1) * 64,
                                                     jc * 512:(jc + 1) * 512],
                                                start=True, stop=True)
                                        return last
                                    return f
                                p.op("t", mkt(), reads=[sbkey, srckey],
                                     writes=[("psbig", qb + jj2)
                                             for jj2 in range(njh)])

                                def mkev(quad=quad, njh=njh, qb=qb, par=par):
                                    lo = quad * 1024
                                    hi = lo + njh * 512

                                    def f(v):
                                        return v.tensor_copy(
                                            qdsb[par][:, lo:hi],
                                            ps_big[:, qb * 512:(qb + njh) * 512])
                                    return f
                                p.op("v", mkev(),
                                     reads=[("psbig", qb + jj2)
                                            for jj2 in range(njh)],
                                     writes=[("qdsb", par, quad)])
                            p.dma(qq, dstd[tslot][lt * 128:(lt + 1) * 128, :],
                                  qdsb[par][:],
                                  reads=[("qdsb", par, hh) for hh in range(nquad)],
                                  writes=[(dkey, tslot, lt)])

            def s3_sweeps(i):
                hp, b = divmod(i, cfg.Bc)
                par = i % 2
                for h2 in range(2):
                    emit_l0_head_sweep(
                        p, c, cfg, b, hp, h2, 2 * par + h2, qdr_d, kdx_d,
                        zrow_d, qT_p[par], kT_p[par], vau[par], rel, stile,
                        ptile, squt, ones_c, ones_r, usml, usmlb, negu, zraw,
                        z0, z0b, ctxsb, ctxn, ctx_d, bvp_sb,
                        ps_big, ps_st, ps_u, ps_bc, pbq, par)

            npair = 8 * cfg.Bc
            s1_proj(0)
            s2_tables(0)
            for i in range(1, npair):
                s1_proj(i)
                s2_tables(i)
                s3_sweeps(i - 1)
            s3_sweeps(npair - 1)
            p.full_barrier()

        with ExitStack() as po:
            wo_sb = po.enter_context(nc.sbuf_tensor([128, 8, 1024], BF16))
            bo_b = po.enter_context(nc.sbuf_tensor([128, 1024], F32))
            ga_b = po.enter_context(nc.sbuf_tensor([128, 1024], F32))
            be_b = po.enter_context(nc.sbuf_tensor([128, 1024], F32))
            ctxl = po.enter_context(nc.sbuf_tensor([128, 8, 128], BF16))
            hsb = po.enter_context(nc.sbuf_tensor([128, 1024], F32))
            ps_po = [po.enter_context(nc.psum_tensor(uname(f"pspo{i}"), [128, 1024], F32))
                     for i in range(2)]
            p.dma("qs", wo_sb[:], d["wo"].rearrange("(db p) e -> p db e", p=128),
                  writes=[("w", "wo")])
            p.dma("qs", bo_b[:], bass.AP(d["bo"].tensor, 0, [[0, 128], [1, 1024]]),
                  writes=[("w", "bo")])
            p.dma("qs", ga_b[:], bass.AP(d["lng"].tensor, 0, [[0, 128], [1, 1024]]),
                  writes=[("w", "ga")])
            p.dma("qs", be_b[:], bass.AP(d["lnb"].tensor, 0, [[0, 128], [1, 1024]]),
                  writes=[("w", "be")])
            for t in range(NT):
                b, lt = t // NS, t % NS
                slot = t % 2
                pp = t % 2
                p.dma("qs", ctxl[:],
                      ctx_d[b, :, :, lt * 128:(lt + 1) * 128]
                      .rearrange("hp p l -> p hp l"),
                      reads=[("ctxd", b, hp2, h22, lt) for hp2 in range(8)
                             for h22 in range(2)],
                      writes=[("ctxl",)])
                p.dma("qs", c.x_sb[slot][:], xin[t * 128:(t + 1) * 128, :],
                      reads=[(xin_tag, t)], writes=[("xsb", slot)])
                p.dma("qs", ybf[:], ytmp_d[t * 128:(t + 1) * 128, :],
                      reads=[("ytmp", t)], writes=[("ybf",)])

                def mko(pp=pp):
                    def f(t_):
                        last = None
                        for nch in range(2):
                            for hp2 in range(8):
                                last = t_.matmul(
                                    ps_po[pp][:, nch * 512:(nch + 1) * 512],
                                    ctxl[:, hp2, :],
                                    wo_sb[:, hp2, nch * 512:(nch + 1) * 512],
                                    start=(hp2 == 0), stop=(hp2 == 7))
                        return last
                    return f
                p.op("t", mko(), reads=[("ctxl",), ("w", "wo")],
                     writes=[("ps", "po", pp)])
                p.op("v", lambda v, pp=pp: v.tensor_tensor(
                        hsb[:], ps_po[pp][:], bo_b[:], ALU.add),
                     reads=[("ps", "po", pp), ("w", "bo")], writes=[("hsb",)])
                p.op("v", lambda v: v.tensor_tensor(hsb[:], hsb[:], ybf[:], ALU.add),
                     reads=[("hsb",), ("ybf",)], writes=[("hsb",)])
                emit_ln_generic(p, c, hsb, ("hsb",), epscol=1)
                p.op("v", lambda v: v.tensor_tensor(c.sc_f32[:], c.sc_f32[:],
                                                    ga_b[:], ALU.mult),
                     reads=[("ysc",), ("w", "ga")], writes=[("ysc",)])
                p.op("v", lambda v: v.tensor_tensor(c.sc_f32[:], c.sc_f32[:],
                                                    be_b[:], ALU.add),
                     reads=[("ysc",), ("w", "be")], writes=[("ysc",)])
                p.op("v", lambda v, sl=slot: v.tensor_tensor(
                        c.xn_sb[sl][:], c.sc_f32[:], c.x_sb[sl][:], ALU.add),
                     reads=[("ysc",), ("xsb", slot)], writes=[("xsb", slot)])
                p.dma("qa", xout[t * 128:(t + 1) * 128, :], c.xn_sb[slot][:],
                      reads=[("xsb", slot)], writes=[(xout_tag, t)])
            p.full_barrier()


def emit_l0_head_sweep(p, c, cfg, b, hp, h2, tslot, qdr_d, kdx_d, zrow_d,
                       qT_p, kT_p, vau, rel, stile, ptile, squt,
                       ones_c, ones_r, usml, usmlb, negu, zraw, z0, z0b,
                       ctxsb, ctxn, ctx_d, bvp_sb,
                       ps_big, ps_st, ps_u, ps_bc, pbq, par=0):
    S, Wp = cfg.S, cfg.Wp
    NRT, NLQ = S // 128, S // 512
    r0h, r1h = h2 * 64, (h2 + 1) * 64

    for srcbuf, sbkey, col in ((qT_p, ("qTp", par), 4), (kT_p, ("kTp", par), 5)):
        p.op("v", lambda v, col=col: v.memset(usml[0:1, col:col + 1], 0.0),
             writes=[("umax", col)])
        for ch in range(S // 512):
            p.op("g", lambda g, ch=ch, srcbuf=srcbuf: g.tensor_tensor(
                    squt[:], srcbuf[:, ch * 512:(ch + 1) * 512],
                    srcbuf[:, ch * 512:(ch + 1) * 512], ALU.mult),
                 reads=[sbkey], writes=[("squt",)])
            p.op("t", lambda t_: t_.matmul(
                    ps_u[0:1, 0:512], ones_c[r0h:r1h, :], squt[r0h:r1h, :],
                    start=True, stop=True),
                 reads=[("squt",), ("onec",)], writes=[("ps", "st", 0)])
            p.op("v", lambda v: v.reduce_max(usml[0:1, 8:9], ps_u[0:1, 0:512],
                                             axis=AX.X),
                 reads=[("ps", "st", 0)], writes=[("um8",)])
            p.op("v", lambda v, col=col: v.tensor_tensor(
                    usml[0:1, col:col + 1], usml[0:1, col:col + 1],
                    usml[0:1, 8:9], ALU.max),
                 reads=[("um8",), ("umax", col)], writes=[("umax", col)])
        p.op("s", lambda s, col=col: s.activation(
                usml[0:1, 12:13], usml[0:1, col:col + 1], AF.Ln),
             reads=[("umax", col)], writes=[("umax", col), ("u12",)])
        p.op("s", lambda s, col=col: s.activation(
                usml[0:1, col + 2:col + 3], usml[0:1, 12:13], AF.Exp,
                bias=0.0, scale=0.5),
             reads=[("umax", col), ("u12",)], writes=[("umax", col)])
    p.op("v", lambda v: v.tensor_tensor(usml[0:1, 9:10], usml[0:1, 6:7],
                                        usml[0:1, 2:3], ALU.add),
         reads=[("umax", 4), ("emax",)], writes=[("usc",)])
    p.op("v", lambda v: v.tensor_tensor(usml[0:1, 10:11], usml[0:1, 7:8],
                                        usml[0:1, 2:3], ALU.add),
         reads=[("umax", 5), ("emax",), ("usc",)], writes=[("usc",)])
    p.op("v", lambda v: v.tensor_tensor(usml[0:1, 11:12], usml[0:1, 9:10],
                                        usml[0:1, 10:11], ALU.mult),
         reads=[("usc",)], writes=[("usc",)])
    p.op("v", lambda v: v.tensor_scalar(usmlb[0:1, 0:1], usml[0:1, 11:12],
                                        -0.125, None, ALU.mult),
         reads=[("usc",)], writes=[("usc",)])
    p.op("t", lambda t_: t_.matmul(ps_u[0:128, 0:1], ones_r[0:1, :],
                                   usmlb[0:1, 0:1], start=True, stop=True),
         reads=[("usc",), ("oner",)], writes=[("ps", "st", 0)])
    p.op("v", lambda v: v.tensor_copy(negu[:], ps_u[:, 0:1]),
         reads=[("ps", "st", 0)], writes=[("negu",)])

    def rel_fetch(rt):
        slot = rt % 3
        r0 = rt * 128
        qsrc = bass.AP(qdr_d[tslot].tensor, (S - 1) + r0,
                       [[Wp - 1, S], [1, 128]])
        p.dma("qs", rel[slot][:], qsrc, transpose=True,
              reads=[("qd", tslot, lt2) for lt2 in range(NRT)],
              writes=[("rel", slot)])
        ksrc = bass.AP(kdx_d[tslot].tensor, (S - 1) + r0 * (Wp - 1),
                       [[Wp - 1, 128], [1, S]])
        p.dma("qg", rel[slot][:], ksrc, accum_op=ALU.add,
              reads=[("kd", tslot, lt2) for lt2 in range(NRT)] + [("rel", slot)],
              writes=[("rel", slot)])

    rel_fetch(0)
    if NRT > 1:
        rel_fetch(1)
    nhb = (NLQ + 1) // 2
    for rt in range(NRT):
        slot = rt % 3
        if rt + 2 < NRT:
            rel_fetch(rt + 2)
        for hb in range(nhb):
            lqs = [lq for lq in (2 * hb, 2 * hb + 1) if lq < NLQ]
            w = len(lqs) * 512

            def mkst(rt=rt, hb=hb, lqs=lqs):
                def f(t_):
                    last = None
                    for i, lq in enumerate(lqs):
                        last = t_.matmul(
                            ps_st[hb][:, i * 512:(i + 1) * 512],
                            kT_p[r0h:r1h, rt * 128:(rt + 1) * 128],
                            qT_p[r0h:r1h, lq * 512:(lq + 1) * 512],
                            start=True, stop=True)
                    return last
                return f
            p.op("t", mkst(), reads=[("kTp", par), ("qTp", par)],
                 writes=[("ps", "st", hb)])
            p.op("v", lambda v, hb=hb, slot=slot, w=w: v.tensor_tensor(
                    stile[:, hb * 1024:hb * 1024 + w], ps_st[hb][:, 0:w],
                    rel[slot][:, hb * 1024:hb * 1024 + w], ALU.add),
                 reads=[("ps", "st", hb), ("rel", slot)], writes=[("stile", hb)])
        p.op("s", lambda s: s.activation(
                ptile[:, 0:S], stile[:, 0:S], AF.Exp,
                bias=negu[:], scale=0.125),
             reads=[("stile", 0), ("stile", 1), ("negu",)], writes=[("pt",)])

        def mkpv(rt=rt):
            def f(t_):
                last = None
                for lq in range(NLQ):
                    last = t_.matmul(
                        ps_big[0:65, lq * 512:(lq + 1) * 512],
                        vau[:, h2, rt, :],
                        ptile[:, lq * 512:(lq + 1) * 512],
                        start=(rt == 0), stop=(rt == NRT - 1))
                return last
            return f
        p.op("t", mkpv(), reads=[("pt",), ("vau", par, h2)],
             writes=[pbq[lq2] for lq2 in range(NLQ)])
    p.op("v", lambda v: v.tensor_copy(zraw[64:65, 0:S], ps_big[64:65, 0:S]),
         reads=pbq[:NLQ], writes=[("zraw",)])
    p.dma("qa", zrow_d[:], zraw[64:65, 0:S], reads=[("zraw",)], writes=[("zrd",)])
    p.dma("qa", z0[:], zrow_d[:], reads=[("zrd",)], writes=[("z0",)])
    p.op("v", lambda v: v.reciprocal(z0[:], z0[:]), reads=[("z0",)],
         writes=[("z0",)])
    p.op("v", lambda v: v.tensor_copy(z0b[:], z0[:]), reads=[("z0",)],
         writes=[("z0b",)])
    p.op("v", lambda v: v.tensor_copy(ctxsb[:, 0:S], ps_big[0:64, 0:S]),
         reads=pbq[:NLQ], writes=[("ctxsb",)])
    for ch in range(S // 512):
        p.op("t", lambda t_, ch=ch: t_.matmul(
                ps_bc[0:64, 0:512], ones_r[0:1, 0:64],
                z0b[0:1, ch * 512:(ch + 1) * 512], start=True, stop=True),
             reads=[("z0b",), ("oner",)], writes=[("ps", "st", 1)])
        p.op("v", lambda v, ch=ch: v.tensor_tensor(
                ctxn[:], ctxsb[:, ch * 512:(ch + 1) * 512], ps_bc[0:64, 0:512],
                ALU.mult),
             reads=[("ps", "st", 1), ("ctxsb",)], writes=[("ctxn",)])
        p.op("v", lambda v: v.tensor_scalar(
                ctxn[:], ctxn[:], bvp_sb[r0h:r1h, hp:hp + 1], None, ALU.add),
             reads=[("ctxn",), ("w", "bvp")], writes=[("ctxn",)])
        p.dma("qa", ctx_d[b, hp, r0h:r1h, ch * 512:(ch + 1) * 512],
              ctxn[:], reads=[("ctxn",)],
              writes=[("ctxd", b, hp, h2, lt2) for lt2 in
                      range(ch * 4, ch * 4 + 4)])


def emit_ln_generic(p, c, src_sb, xkey, epscol=0):
    st = c.st_small
    kst = ("lnst",)
    p.op("v", lambda v: v.bn_stats(st[:, 0:6], src_sb[:, 0:512]),
         reads=[xkey], writes=[kst])
    p.op("v", lambda v: v.bn_stats(st[:, 6:12], src_sb[:, 512:1024]),
         reads=[xkey], writes=[kst])
    p.op("v", lambda v: v.bn_aggr(st[:, 12:14], st[:, 0:12]), reads=[kst], writes=[kst])
    p.op("s", lambda s: s.activation(st[:, 14:15], st[:, 13:14], AF.Ln,
                                     bias=c.consts[:, epscol:epscol + 1], scale=1.0),
         reads=[kst, ("c", "k")], writes=[kst])
    p.op("s", lambda s: s.activation(st[:, 15:16], st[:, 14:15], AF.Exp,
                                     bias=0.0, scale=-0.5),
         reads=[kst], writes=[kst])
    p.op("v", lambda v: v.tensor_scalar(st[:, 16:17], st[:, 12:13], -1.0, None, ALU.mult),
         reads=[kst], writes=[kst])
    p.op("v", lambda v: v.tensor_tensor(st[:, 17:18], st[:, 16:17], st[:, 15:16], ALU.mult),
         reads=[kst], writes=[kst])
    p.op("s", lambda s: s.activation(c.sc_f32[:], src_sb[:], AF.Identity,
                                     bias=st[:, 17:18], scale=st[:, 15:16]),
         reads=[xkey, kst], writes=[("ysc",)])


def emit_phase_l1attn(p, stk, nc, c, cfg, xin, xout, xin_tag, xout_tag):
    d = c.d
    S, T, NT, NS = cfg.S, cfg.T, cfg.NT, cfg.NS
    LB, TK = cfg.LB, cfg.TK
    sband_d = [nc.dram_tensor(f"sbandd{i}", [128, 3 * 128], F32, kind="Internal").ap()
               for i in range(2)]
    w2r_d = [nc.dram_tensor(f"w2rd{i}", [128, 2 * LB], BF16, kind="Internal").ap()
             for i in range(2)]
    phr_d = [nc.dram_tensor(f"phrd{i}", [1, 128], F32, kind="Internal").ap()
             for i in range(2)]

    with ExitStack() as ph:
        wsb = {}
        for nm in ("q", "k", "pk", "v", "pv"):
            wsb[nm] = ph.enter_context(
                nc.sbuf_tensor(uname(f"l1w_{nm}"), [128, 8, 1024], BF16))
            p.dma("qs", wsb[nm][:],
                  d[nm + "wT"].rearrange("(jb p) i -> p jb i", p=128),
                  writes=[("w", nm)])
        qbp_sb = ph.enter_context(nc.sbuf_tensor([128, 8], F32))
        kbp_sb = ph.enter_context(nc.sbuf_tensor([128, 8], F32))
        pkbp_sb = ph.enter_context(nc.sbuf_tensor([128, 8], F32))
        p.dma("qs", qbp_sb[:], d["qbp"], writes=[("w", "qbp")])
        p.dma("qs", kbp_sb[:], d["kbp"], writes=[("w", "kbp")])
        p.dma("qs", pkbp_sb[:], d["pkbp"], writes=[("w", "pkbp")])
        vb_b = ph.enter_context(nc.sbuf_tensor([128, 1024], F32))
        pvb_b = ph.enter_context(nc.sbuf_tensor([128, 1024], F32))
        p.dma("qs", vb_b[:], bass.AP(d["vbr"].tensor, 0, [[0, 128], [1, 1024]]),
              writes=[("w", "vb")])
        p.dma("qs", pvb_b[:], bass.AP(d["pvbr"].tensor, 0, [[0, 128], [1, 1024]]),
              writes=[("w", "pvb")])
        lm_sb = ph.enter_context(nc.sbuf_tensor([128, 128], BF16))
        p.dma("qs", lm_sb[:], d["lmatT"], writes=[("w", "lm")])
        vm_sb = [ph.enter_context(nc.sbuf_tensor(uname(f"vmsb{i}"), [128, LB], F32))
                 for i in range(2)]
        p.dma("qs", vm_sb[0][:], d["vm0"], writes=[("w", "vm0")])
        p.dma("qs", vm_sb[1][:], d["vm1"], writes=[("w", "vm1")])
        vmn_sb = [ph.enter_context(nc.sbuf_tensor(uname(f"vmn{i}"), [128, LB], F32))
                  for i in range(2)]
        p.dma("qs", vmn_sb[0][:], d["vmn0"], writes=[("w", "vmn0")])
        p.dma("qs", vmn_sb[1][:], d["vmn1"], writes=[("w", "vmn1")])
        zpad = ph.enter_context(nc.sbuf_tensor([128, 2 * LB], BF16))
        p.op("v", lambda v: v.memset(zpad[:], 0.0), writes=[("zpad",)])
        for i in range(2):
            p.dma("qa", w2r_d[i][:], zpad[:], reads=[("zpad",)],
                  writes=[("w2rd", i)])

        ytile = ph.enter_context(nc.sbuf_tensor([128, 1024], BF16))
        yT = ph.enter_context(nc.sbuf_tensor([128, 8, 128], BF16))
        phf = ph.enter_context(nc.sbuf_tensor([128, 1024], F32))
        phT = ph.enter_context(nc.sbuf_tensor([128, 8, 128], BF16))
        qT_t = ph.enter_context(nc.sbuf_tensor([128, 8, 128], BF16))
        kT_r = [ph.enter_context(nc.sbuf_tensor(uname(f"ktr{i}"), [128, 8, 128], BF16))
                for i in range(4)]
        v_r = [ph.enter_context(nc.sbuf_tensor(uname(f"vr{i}"), [128, 1024], BF16))
               for i in range(4)]
        pkT_r = [ph.enter_context(nc.sbuf_tensor(uname(f"pktr{i}"), [128, 8, 128], BF16))
                 for i in range(3)]
        pv_r = [ph.enter_context(nc.sbuf_tensor(uname(f"pvr{i}"), [128, 1024], BF16))
                for i in range(3)]
        for i in range(4):
            p.op("v", lambda v, i=i: v.memset(kT_r[i][:], 0.0), writes=[("ktr", i)])
            p.op("v", lambda v, i=i: v.memset(v_r[i][:], 0.0), writes=[("vr", i)])
        for i in range(3):
            p.op("v", lambda v, i=i: v.memset(pkT_r[i][:], 0.0), writes=[("pktr", i)])
            p.op("v", lambda v, i=i: v.memset(pv_r[i][:], 0.0), writes=[("pvr", i)])
        pkprev = ph.enter_context(nc.sbuf_tensor([128, 8, 128], BF16))
        pvprev = ph.enter_context(nc.sbuf_tensor([128, 1024], BF16))
        tq = ph.enter_context(nc.sbuf_tensor([128, 1024], BF16))
        sband_sb = ph.enter_context(nc.sbuf_tensor([128, 3 * 128], F32))
        s2r = ph.enter_context(nc.sbuf_tensor([128, LB], F32))
        wrk = ph.enter_context(nc.sbuf_tensor([128, LB], F32))
        expv = ph.enter_context(nc.sbuf_tensor([128, LB], F32))
        em = ph.enter_context(nc.sbuf_tensor([128, LB], F32))
        w2r = ph.enter_context(nc.sbuf_tensor([128, LB], BF16))
        wsk = ph.enter_context(nc.sbuf_tensor([128, 3 * 128], BF16))
        wT_sb = ph.enter_context(nc.sbuf_tensor([128, 3, 128], BF16))
        phrr = [ph.enter_context(nc.sbuf_tensor(uname(f"phrr{i}"), [1, 128], F32))
                for i in range(2)]
        phrc = [ph.enter_context(nc.sbuf_tensor(uname(f"phrc{i}"), [128, 1], F32))
                for i in range(2)]
        st2 = ph.enter_context(nc.sbuf_tensor([128, 8], F32))
        ones_c = ph.enter_context(nc.sbuf_tensor([128, 1], BF16))
        p.op("v", lambda v: v.memset(ones_c[:], 1.0), writes=[("onec",)])
        mr8 = ph.enter_context(nc.sbuf_tensor([128, 8], F32))
        p.op("v", lambda v: v.memset(mr8[:], 3.0e38), writes=[("mr8",)])
        out_sb = ph.enter_context(nc.sbuf_tensor([128, 1024], F32))

        with ExitStack() as mn:
            ps_t = mn.enter_context(nc.psum_tensor([128, 1024], F32))
            ps_band = mn.enter_context(nc.psum_tensor([128, 3 * 128], F32))
            ps_pp = [mn.enter_context(nc.psum_tensor(uname(f"l1pp{i}"), [128, 512], F32))
                     for i in range(2)]
            ps_o = mn.enter_context(nc.psum_tensor([128, 1024], F32))
            ps_wt = mn.enter_context(nc.psum_tensor([128, 3 * 128], BF16))

            def l1_stageA(gt):
                sq, t = divmod(gt, NS)
                slot = gt % 2
                kcur, kp1, kp2 = gt % 4, (gt - 1) % 4, (gt - 2) % 4
                pcur, pprev = gt % 3, (gt - 1) % 3
                p.dma("qs", c.x_sb[slot][:], xin[gt * 128:(gt + 1) * 128, :],
                      reads=[(xin_tag, gt)], writes=[("xsb", slot)])
                emit_ln(p, c, slot, ("xsb", slot))
                p.op("v", lambda v: v.tensor_copy(ytile[:], c.sc_f32[:]),
                     reads=[("ysc",)], writes=[("ytile",)])

                def cp_b(s):
                    return s.copy(yT[:],
                                  ps_t[:].rearrange("p (cc n) -> p cc n", cc=8))
                emit_transpose_f32(p, c, c.sc_f32, ("ysc",), ps_t,
                                   [("s", cp_b, [], [("yTt",)])])
                p.op("t", lambda t_: [
                        t_.matmul(ps_t[:, 0:512], lm_sb[:], ytile[:, 0:512],
                                  start=True, stop=True),
                        t_.matmul(ps_t[:, 512:1024], lm_sb[:],
                                  ytile[:, 512:1024], start=True, stop=True)][-1],
                     reads=[("w", "lm"), ("ytile",)], writes=[("ps", "tr")])
                p.op("v", lambda v: v.tensor_copy(phf[:], ps_t[:]),
                     reads=[("ps", "tr")], writes=[("phf",)])

                def cp_ph(s):
                    return s.copy(phT[:],
                                  ps_t[:].rearrange("p (cc n) -> p cc n", cc=8))
                emit_transpose_f32(p, c, phf, ("phf",), ps_t,
                                   [("s", cp_ph, [], [("phT",)])])
                for nm, outbuf, okey, bp_sb, bpkey, rhsbuf, rhskey in (
                        ("q", qT_t, ("qTt",), qbp_sb, ("w", "qbp"), yT, ("yTt",)),
                        ("k", kT_r[kcur], ("ktr", kcur), kbp_sb, ("w", "kbp"),
                         yT, ("yTt",)),
                        ("pk", pkT_r[pcur], ("pktr", pcur), pkbp_sb,
                         ("w", "pkbp"), phT, ("phT",))):
                    for ib in range(8):
                        pp = ib % 2

                        def mkq(nm=nm, ib=ib, pp=pp, rhsbuf=rhsbuf):
                            def f(t_):
                                last = None
                                for jb in range(8):
                                    last = t_.matmul(
                                        ps_pp[pp][:, 0:128],
                                        wsb[nm][:, jb, ib * 128:(ib + 1) * 128],
                                        rhsbuf[:, jb, :],
                                        start=(jb == 0), stop=(jb == 7))
                                return last
                            return f
                        p.op("t", mkq(), reads=[("w", nm), rhskey],
                             writes=[("ps", "pp", pp)])
                        p.op("v", lambda v, outbuf=outbuf, ib=ib, pp=pp,
                             bp_sb=bp_sb: v.tensor_scalar(
                                outbuf[:, ib, :], ps_pp[pp][:, 0:128],
                                bp_sb[:, ib:ib + 1], None, ALU.add),
                             reads=[("ps", "pp", pp), bpkey], writes=[okey])
                for nm, ring, rkey, bb, bbkey, lhsbuf, lhskey in (
                        ("v", v_r[kcur], ("vr", kcur), vb_b, ("w", "vb"),
                         yT, ("yTt",)),
                        ("pv", pv_r[pcur], ("pvr", pcur), pvb_b, ("w", "pvb"),
                         phT, ("phT",))):
                    def mkvm(nm=nm, lhsbuf=lhsbuf):
                        def f(t_):
                            last = None
                            for nch in range(2):
                                for jb in range(8):
                                    last = t_.matmul(
                                        ps_o[:, nch * 512:(nch + 1) * 512],
                                        lhsbuf[:, jb, :],
                                        wsb[nm][:, jb, nch * 512:(nch + 1) * 512],
                                        start=(jb == 0), stop=(jb == 7))
                            return last
                        return f
                    p.op("t", mkvm(), reads=[("w", nm), lhskey],
                         writes=[("ps", "o")])
                    p.op("v", lambda v, ring=ring, bb=bb: v.tensor_tensor(
                            ring[:], ps_o[:], bb[:], ALU.add),
                         reads=[("ps", "o"), bbkey], writes=[rkey])
                def mkband(t=t, kcur=kcur, kp1=kp1, kp2=kp2):
                    ring_idx = [kp2, kp1, kcur]

                    def f(t_):
                        last = None
                        for w in range(3):
                            for jb in range(8):
                                last = t_.matmul(
                                    ps_band[:, w * 128:(w + 1) * 128],
                                    qT_t[:, jb, :],
                                    kT_r[ring_idx[w]][:, jb, :],
                                    start=(jb == 0), stop=(jb == 7))
                        return last
                    return f
                p.op("t", mkband(),
                     reads=[("qTt",), ("ktr", kcur), ("ktr", kp1), ("ktr", kp2)],
                     writes=[("ps", "band")])
                p.op("v", lambda v: v.tensor_copy(sband_sb[:], ps_band[:]),
                     reads=[("ps", "band")], writes=[("sband",)])
                p.dma("qa", sband_d[slot][:], sband_sb[:],
                      reads=[("sband",)], writes=[("sbandd", slot)])
                p.op("v", lambda v, pcur=pcur: v.tensor_copy(
                        pkprev[:, :, 1:128], pkT_r[pcur][:, :, 0:127]),
                     reads=[("pktr", pcur)], writes=[("pkprev",)])
                if t == 0:
                    p.op("v", lambda v: v.memset(pkprev[:, :, 0:1], 0.0),
                         reads=[], writes=[("pkprev",)])
                else:
                    p.op("v", lambda v, pprev=pprev: v.tensor_copy(
                            pkprev[:, :, 0:1], pkT_r[pprev][:, :, 127:128]),
                         reads=[("pktr", pprev)], writes=[("pkprev",)])
                p.op("v", lambda v: v.tensor_tensor(tq[:],
                                                    qT_t[:].rearrange(
                                                        "p a b -> p (a b)"),
                                                    pkprev[:].rearrange(
                                                        "p a b -> p (a b)"),
                                                    ALU.mult),
                     reads=[("qTt",), ("pkprev",)], writes=[("tq",)])

                def mkphr(t_):
                    last = None
                    for jb in range(8):
                        last = t_.matmul(ps_band[0:1, 0:128], ones_c[:],
                                         tq[:, jb * 128:(jb + 1) * 128],
                                         start=(jb == 0), stop=(jb == 7))
                    return last
                p.op("t", mkphr, reads=[("tq",), ("onec",)],
                     writes=[("ps", "band")])
                p.op("v", lambda v: v.tensor_copy(phrr[slot][:], ps_band[0:1, 0:128]),
                     reads=[("ps", "band")], writes=[("phrr", slot)])
                p.dma("qa", phr_d[slot][:], phrr[slot][:],
                      reads=[("phrr", slot)], writes=[("phrd", slot)])
                p.dma("qa", phrc[slot][:],
                      bass.AP(phr_d[slot].tensor, 0, [[1, 128], [1, 1]]),
                      reads=[("phrd", slot)], writes=[("phrc", slot)])

            def l1_stageB(gt):
                sq, t = divmod(gt, NS)
                slot = gt % 2
                kcur, kp1, kp2 = gt % 4, (gt - 1) % 4, (gt - 2) % 4
                pcur, pprev = gt % 3, (gt - 1) % 3
                skewsrc = bass.AP(sband_d[slot].tensor, 0, [[385, 128], [1, LB]])
                p.dma("qs", s2r[:], skewsrc,
                      reads=[("sbandd", slot)], writes=[("s2r",)])
                if t < 2:
                    p.op("v", lambda v, t=t: v.tensor_tensor(
                            s2r[:], s2r[:], vm_sb[t][:], ALU.mult),
                         reads=[("s2r",), ("w", f"vm{t}")], writes=[("s2r",)])
                    p.op("v", lambda v, t=t: v.tensor_tensor(
                            s2r[:], s2r[:], vmn_sb[t][:], ALU.add),
                         reads=[("s2r",), ("w", f"vmn{t}")], writes=[("s2r",)])
                p.op("v", lambda v: v.reduce_max(st2[:, 0:1], s2r[:], axis=AX.X),
                     reads=[("s2r",)], writes=[("st2",)])
                p.op("v", lambda v: v.tensor_copy(mr8[:, 0:1], st2[:, 0:1]),
                     reads=[("st2",)], writes=[("mr8",)])
                p.op("v", lambda v: v.match_replace(wrk[:], mr8[:], s2r[:],
                                                    NEG),
                     reads=[("s2r",), ("mr8",)], writes=[("wrk",)])
                for it in range(1, TK - 1):
                    p.op("v", lambda v: v.reduce_max(mr8[:, 0:1], wrk[:],
                                                     axis=AX.X),
                         reads=[("wrk",)], writes=[("mr8",)])
                    p.op("v", lambda v: v.match_replace(wrk[:], mr8[:],
                                                        wrk[:], NEG),
                         reads=[("wrk",), ("mr8",)], writes=[("wrk",)])
                p.op("v", lambda v: v.reduce_max(st2[:, 1:2], wrk[:], axis=AX.X),
                     reads=[("wrk",)], writes=[("st2",)])
                p.op("v", lambda v: v.tensor_scalar(
                        st2[:, 2:3], phrc[slot][:], 1.0 / 32.0, None, ALU.mult),
                     reads=[("phrc", slot)], writes=[("st2",)])
                p.op("v", lambda v: v.tensor_tensor(
                        st2[:, 3:4], st2[:, 0:1], phrc[slot][:], ALU.max),
                     reads=[("st2",), ("phrc", slot)], writes=[("st2",)])
                p.op("v", lambda v: v.tensor_scalar(
                        st2[:, 4:5], st2[:, 3:4], -1.0 / 32.0, None, ALU.mult),
                     reads=[("st2",)], writes=[("st2",)])
                p.op("s", lambda s: s.activation(expv[:], s2r[:], AF.Exp,
                                                 bias=st2[:, 4:5],
                                                 scale=1.0 / 32.0),
                     reads=[("s2r",), ("st2",)], writes=[("expv",)])
                p.op("v", lambda v: v.tensor_scalar(em[:], s2r[:], st2[:, 1:2],
                                                    None, ALU.is_ge),
                     reads=[("s2r",), ("st2",)], writes=[("em",)])
                p.op("v", lambda v: v.tensor_tensor(em[:], em[:], expv[:],
                                                    ALU.mult),
                     reads=[("em",), ("expv",)], writes=[("em",)])
                p.op("v", lambda v: v.reduce_sum(st2[:, 5:6], em[:], axis=AX.X),
                     reads=[("em",)], writes=[("st2",)])
                p.op("s", lambda s: s.activation(st2[:, 6:7], phrc[slot][:], AF.Exp,
                                                 bias=st2[:, 4:5],
                                                 scale=1.0 / 32.0),
                     reads=[("phrc", slot), ("st2",)], writes=[("st2",)])
                p.op("v", lambda v: v.tensor_tensor(
                        st2[:, 5:6], st2[:, 5:6], expv[:, LB - 1:LB], ALU.add),
                     reads=[("st2",), ("expv",)], writes=[("st2",)])
                p.op("v", lambda v: v.tensor_tensor(
                        st2[:, 5:6], st2[:, 5:6], st2[:, 6:7], ALU.add),
                     reads=[("st2",)], writes=[("st2",)])
                p.op("v", lambda v: v.reciprocal(st2[:, 7:8], st2[:, 5:6]),
                     reads=[("st2",)], writes=[("st2",)])
                p.op("v", lambda v: v.tensor_scalar(w2r[:], em[:], st2[:, 7:8],
                                                    None, ALU.mult),
                     reads=[("em",), ("st2",)], writes=[("w2r",)])
                p.op("v", lambda v: v.scalar_tensor_tensor(
                        w2r[:, LB - 1:LB], expv[:, LB - 1:LB], st2[:, 7:8],
                        w2r[:, LB - 1:LB], ALU.mult, ALU.add),
                     reads=[("expv",), ("st2",), ("w2r",)], writes=[("w2r",)])
                p.op("v", lambda v: v.tensor_tensor(st2[:, 6:7], st2[:, 6:7],
                                                    st2[:, 7:8], ALU.mult),
                     reads=[("st2",)], writes=[("st2",)])
                p.dma("qa", w2r_d[slot][:, 128:128 + LB], w2r[:],
                      reads=[("w2r",)], writes=[("w2rd", slot)])
                wsksrc = bass.AP(w2r_d[slot].tensor, 128, [[511, 128], [1, 384]])
                p.dma("qs", wsk[:], wsksrc,
                      reads=[("w2rd", slot)], writes=[("wsk",)])
                p.op("t", lambda t_: [
                        t_.transpose(ps_wt[:, w * 128:(w + 1) * 128],
                                     wsk[:, w * 128:(w + 1) * 128],
                                     c.ident_b[:]) for w in range(3)][-1],
                     reads=[("wsk",), ("c", "idb")], writes=[("ps", "wt")])
                p.op("v", lambda v: v.tensor_copy(
                        wT_sb[:].rearrange("p a b -> p (a b)"), ps_wt[:]),
                     reads=[("ps", "wt")], writes=[("wTsb",)])
                def mkwm(kcur=kcur, kp1=kp1, kp2=kp2):
                    ring_idx = [kp2, kp1, kcur]

                    def f(t_):
                        last = None
                        for nch in range(2):
                            for w in range(3):
                                last = t_.matmul(
                                    ps_o[:, nch * 512:(nch + 1) * 512],
                                    wT_sb[:, w, :],
                                    v_r[ring_idx[w]][:, nch * 512:(nch + 1) * 512],
                                    start=(w == 0), stop=(w == 2))
                        return last
                    return f
                p.op("t", mkwm(),
                     reads=[("wTsb",), ("vr", kcur), ("vr", kp1), ("vr", kp2)],
                     writes=[("ps", "o")])
                p.dma("qg", pvprev[1:128, :], pv_r[pcur][0:127, :],
                      reads=[("pvr", pcur)], writes=[("pvprev",)])
                if t == 0:
                    p.op("v", lambda v: v.memset(pvprev[0:1, :], 0.0),
                         writes=[("pvprev",)])
                else:
                    p.dma("qg", pvprev[0:1, :], pv_r[pprev][127:128, :],
                          reads=[("pvr", pprev)], writes=[("pvprev",)])
                p.op("v", lambda v: v.scalar_tensor_tensor(
                        out_sb[:], pvprev[:], st2[:, 6:7], ps_o[:],
                        ALU.mult, ALU.add),
                     reads=[("pvprev",), ("st2",), ("ps", "o")],
                     writes=[("outsb",)])
                if t == 0:
                    p.op("v", lambda v: v.memset(out_sb[0:1, :], 0.0),
                         writes=[("outsb",)])
                p.op("v", lambda v, sl=slot: v.tensor_tensor(
                        c.xn_sb[sl][:], out_sb[:], c.x_sb[sl][:], ALU.add),
                     reads=[("outsb",), ("xsb", slot)], writes=[("xsb", slot)])
                p.dma("qa", xout[gt * 128:(gt + 1) * 128, :], c.xn_sb[slot][:],
                      reads=[("xsb", slot)], writes=[(xout_tag, gt)])

            NTILE = cfg.Bc * NS
            l1_stageA(0)
            for gt in range(1, NTILE):
                l1_stageA(gt)
                l1_stageB(gt - 1)
            l1_stageB(NTILE - 1)
            p.full_barrier()


def emit_bf16_conversion(p, stk, nc, names, conv_list):
    with ExitStack() as cv:
        cf = [cv.enter_context(nc.sbuf_tensor(uname(f"cvf{i}"), [128, 4096], F32))
              for i in range(2)]
        cb = [cv.enter_context(nc.sbuf_tensor(uname(f"cvb{i}"), [128, 4096], BF16))
              for i in range(2)]
        for nm in conv_list:
            srcap = names[nm]
            shape = list(srcap.shape)
            nelem = int(np.prod(shape))
            dstt = nc.dram_tensor(uname(nm + "B"), shape, BF16, kind="Internal")
            dstap = dstt.ap()
            off = 0
            i = 0
            while off < nelem:
                w = min(4096, (nelem - off) // 128)
                assert w > 0 and (nelem - off) % 128 == 0
                sl = i % 2
                sap = bass.AP(srcap.tensor, off, [[w, 128], [1, w]])
                dap = bass.AP(dstap.tensor, off, [[w, 128], [1, w]])
                p.dma("qs", cf[sl][:, 0:w], sap, writes=[("cvf", sl)])
                eng = "v" if i % 2 == 0 else "s"
                if eng == "v":
                    p.op("v", lambda v, sl=sl, w=w: v.tensor_copy(
                            cb[sl][:, 0:w], cf[sl][:, 0:w]),
                         reads=[("cvf", sl)], writes=[("cvb", sl)])
                else:
                    p.op("s", lambda s, sl=sl, w=w: s.copy(
                            cb[sl][:, 0:w], cf[sl][:, 0:w]),
                         reads=[("cvf", sl)], writes=[("cvb", sl)])
                p.dma("qa", dap, cb[sl][:, 0:w], reads=[("cvb", sl)],
                      writes=[("cvout",)])
                off += 128 * w
                i += 1
            names[nm] = dstap
    p.full_barrier()


PHASES = {}


def _register_phases():
    PHASES["moe"] = emit_phase_moe
    PHASES["l1ffn"] = emit_phase_l1ffn
    try:
        PHASES["l0attn"] = emit_phase_l0attn
        PHASES["l1attn"] = emit_phase_l1attn
    except NameError:
        pass


def build_program(cfg, phase_names, input_specs):
    nc = bass.Bass("TRN2", target_bir_lowering=False, debug=False)
    names = {}
    for nm, (shape, dt) in input_specs.items():
        names[nm] = nc.dram_tensor(nm, list(shape), dt, kind="ExternalInput").ap()
    xin = nc.dram_tensor("xin", [cfg.T, cfg.D], F32, kind="ExternalInput").ap()
    xout = nc.dram_tensor("xout", [cfg.T, cfg.D], F32, kind="ExternalOutput").ap()
    xa = nc.dram_tensor("xa", [cfg.T, cfg.D], F32, kind="Internal").ap()
    xb = nc.dram_tensor("xb", [cfg.T, cfg.D], F32, kind="Internal").ap()
    nph = len(phase_names)
    chain = []
    cur = (xin, "XIN")
    for i, pn in enumerate(phase_names):
        if i == nph - 1:
            nxt = (xout, "XOUT")
        elif i % 2 == 0:
            nxt = (xa, "XA")
        else:
            nxt = (xb, "XB")
        chain.append((pn, cur, nxt))
        cur = nxt

    with ExitStack() as stk:
        sems = {}
        for k in ("t", "v", "s", "g"):
            sems[k] = stk.enter_context(nc.semaphore(f"sem_{k}"))
        for q in ("qs", "qa", "qg"):
            for i in range(Prog.NSUB):
                sems[(q, i)] = stk.enter_context(nc.semaphore(f"sem_{q}{i}"))
        block = stk.enter_context(nc.Block())
        p = Prog(nc, block, sems)
        conv_list = ["fc", "pjT", "wq", "wk", "wv", "wo",
                     "qwT", "kwT", "vwT", "pkwT", "pvwT",
                     "win", "wouts", "extk", "exrtk"]
        emit_bf16_conversion(p, stk, nc, names, conv_list)
        c = alloc_common(stk, p, nc, cfg, names)
        c.nc = nc
        c.cfg = cfg
        for pn, (srcx, stag), (dst, dtag) in chain:
            PHASES[pn](p, stk, nc, c, cfg, srcx, dst, stag, dtag)
            p.full_barrier()
        p.drain()
    return nc


def _stack2(a):
    return np.concatenate([a, a], axis=0)


def prep_weight_inputs(cfg, I):
    P = {}
    P["rwT"] = np.ascontiguousarray(I["L0_ffn_router_w"].T)
    P["fc"] = np.ascontiguousarray(I["L0_ffn_expert_fc"])
    P["pjT"] = np.ascontiguousarray(I["L0_ffn_expert_proj"].transpose(0, 2, 1))
    P["wq"] = np.ascontiguousarray(I["L0_attn_Wq"])
    P["wk"] = np.ascontiguousarray(I["L0_attn_Wk"])
    P["wv"] = np.ascontiguousarray(I["L0_attn_Wv"])
    P["wo"] = np.ascontiguousarray(I["L0_attn_Wo"])
    P["bqp"] = np.ascontiguousarray(I["L0_attn_bq"].reshape(8, 128).T)
    P["bkp"] = np.ascontiguousarray(I["L0_attn_bk"].reshape(8, 128).T)
    P["bvp"] = np.ascontiguousarray(I["L0_attn_bv"].reshape(8, 128).T)
    P["bo"] = I["L0_attn_bo"].reshape(1, 1024)
    P["lng"] = I["L0_attn_ln_gamma"].reshape(1, 1024)
    P["lnb"] = I["L0_attn_ln_beta"].reshape(1, 1024)
    S, M = cfg.S, cfg.M
    de = I["L0_attn_dist_emb"]
    jp = np.arange(cfg.Wp)
    idx = np.clip(jp - (S - 1) + (M - 1), 0, 2 * M - 2)
    ext = np.ascontiguousarray(de[idx].T)
    idx_r = np.clip((2 * S - 2) - jp - (S - 1) + (M - 1), 0, 2 * M - 2)
    exrt = np.ascontiguousarray(de[idx_r].T)
    P["extk"] = np.ascontiguousarray(np.concatenate([ext, ext], axis=0))
    P["exrtk"] = np.ascontiguousarray(np.concatenate([exrt, exrt], axis=0))
    for nm in ("q", "k", "v", "pk", "pv"):
        P[nm + "wT"] = np.ascontiguousarray(I[f"L1_attn_{nm}_w"].T)
    P["qbp"] = np.ascontiguousarray(I["L1_attn_q_b"].reshape(8, 128).T)
    P["kbp"] = np.ascontiguousarray(I["L1_attn_k_b"].reshape(8, 128).T)
    P["pkbp"] = np.ascontiguousarray(I["L1_attn_pk_b"].reshape(8, 128).T)
    P["vbr"] = I["L1_attn_v_b"].reshape(1, 1024)
    P["pvbr"] = I["L1_attn_pv_b"].reshape(1, 1024)
    P["win"] = np.ascontiguousarray(I["L1_ffn_Win"])
    P["bins"] = _stack2(I["L1_ffn_bin_"]).reshape(128, 1)
    P["w1s"] = np.ascontiguousarray(_stack2(I["L1_ffn_W1"]))
    P["b1s"] = np.ascontiguousarray(_stack2(I["L1_ffn_b1"]))
    P["w2s"] = np.ascontiguousarray(_stack2(I["L1_ffn_W2"]))
    P["b2s"] = _stack2(I["L1_ffn_b2"]).reshape(128, 1)
    P["wouts"] = np.ascontiguousarray(_stack2(I["L1_ffn_Wout"]))
    P["bout"] = I["L1_ffn_bout"].reshape(1, 1024)
    P["ident_f"] = np.eye(128, dtype=np.float32)
    P["ident_b"] = np.eye(128, dtype=BFDT)
    lm = np.zeros((128, 128), dtype=np.float32)
    for blk in range(128 // cfg.CH):
        for i in range(cfg.CH):
            for i2 in range(i + 1):
                lm[blk * cfg.CH + i2, blk * cfg.CH + i] = 1.0 / (i + 1)
    P["lmatT"] = lm.astype(BFDT)
    sr = np.arange(128)[:, None]
    dp = np.arange(cfg.LB)[None, :]
    P["vm0"] = ((sr + dp) >= cfg.LB).astype(np.float32)
    P["vm1"] = ((sr + dp) >= cfg.LB - 128).astype(np.float32)
    P["vmn0"] = (NEG * (1.0 - P["vm0"])).astype(np.float32)
    P["vmn1"] = (NEG * (1.0 - P["vm1"])).astype(np.float32)
    P["ones_row"] = np.ones((1, 128), dtype=BFDT)
    return P


def input_specs_of(P):
    out = {}
    for k, v in P.items():
        dt = {np.dtype(np.float32): F32, np.dtype(BFDT): BF16}[v.dtype]
        out[k] = (v.shape, dt)
    return out


def kernel(**inputs):
    _register_phases()
    cfg = CFG(S=2048, M=1024)
    x = np.ascontiguousarray(inputs["L0_attn_hidden_states"], dtype=np.float32)
    B, S, D = x.shape
    assert (B, S, D) == (16, 2048, 1024)
    P = prep_weight_inputs(cfg, inputs)
    specs = input_specs_of(P)
    phase_seq = ["moe", "l0attn", "moe", "l1ffn", "l1attn", "l1ffn"]
    fuse = os.environ.get("ATHENA_UNFUSED", "0") != "1"
    shards = [np.ascontiguousarray(x[2 * i:2 * i + 2].reshape(cfg.T, cfg.D))
              for i in range(8)]
    if fuse:
        nc = build_program(cfg, phase_seq, specs)
        in_maps = [dict(P, xin=shards[i]) for i in range(8)]
        res = run_bass_kernel_spmd(nc, in_maps, core_ids=list(range(8)))
        outs = [r["xout"] for r in res.results]
    else:
        cur = shards
        for pn in phase_seq:
            nc = build_program(cfg, [pn], specs)
            in_maps = [dict(P, xin=cur[i]) for i in range(8)]
            res = run_bass_kernel_spmd(nc, in_maps, core_ids=list(range(8)))
            cur = [np.asarray(r["xout"], dtype=np.float32) for r in res.results]
        outs = cur
    return np.concatenate([o.reshape(2, S, D) for o in outs], axis=0)


# version 15 (final)
# speedup vs baseline: 1.0204x; 1.0204x over previous
import os
import numpy as np
import ml_dtypes
from contextlib import ExitStack

import concourse.bass as bass
import concourse.mybir as mybir
from concourse.bass_utils import run_bass_kernel_spmd

F32 = mybir.dt.float32
BF16 = mybir.dt.bfloat16
AF = mybir.ActivationFunctionType
ALU = mybir.AluOpType
AX = mybir.AxisListType
BFDT = ml_dtypes.bfloat16

NEG = -60000.0

_NAME_CTR = [0]


def uname(base):
    _NAME_CTR[0] += 1
    return f"{base}_u{_NAME_CTR[0]}"


class CFG:
    def __init__(self, S=2048, M=1024):
        self.S = S
        self.M = M
        self.Bc = 2
        self.T = self.Bc * S
        self.NT = self.T // 128
        self.NS = S // 128
        self.D = 1024
        self.DC = 8
        self.E = 8
        self.HD = 4096
        self.H = 16
        self.Dh = 64
        self.Wp = 2 * S
        self.LB = 256
        self.TK = 16
        self.CH = 32
        self.NW = self.LB // 128 + 1


class Prog:
    DMA_QUEUES = ("qs", "qa", "qg")
    NSUB = 4

    def __init__(self, nc, block, sems):
        self.nc, self.block = nc, block
        self.streams = {
            "t": "tensor", "v": "vector", "s": "scalar", "g": "gpsimd",
            "qs": "sync", "qa": "scalar", "qg": "gpsimd",
        }
        for q in self.DMA_QUEUES:
            for i in range(self.NSUB):
                self.streams[(q, i)] = self.streams[q]
        self.sems = sems
        self.qidx = {q: 0 for q in self.DMA_QUEUES}
        self.cnt = {k: 0 for k in self.streams}
        self.watermark = {}
        self.last_w = {}
        self.readers = {}
        self.pend = {s: [] for s in set(self.streams.values())}

    MAX_BLOCK = int(os.environ.get('ATHENA_MAXBLK', '512'))

    PACKED = ("tensor", "vector", "scalar", "sync")

    def _flush_stream(self, stream):
        q = self.pend[stream]
        if q:
            def runner(h, fns=tuple(q)):
                for g in fns:
                    g(h)
            getattr(self.block, stream)(runner)
            self.pend[stream] = []

    def _emit(self, stream, f, is_dma=False):
        self.pend[stream].append(f)
        self._flush_stream(stream)

    def flush(self):
        for stream in sorted(self.pend):
            self._flush_stream(stream)

    def op(self, ek, emit, reads=(), writes=(), inc=None):
        if ek in self.DMA_QUEUES:
            sub = self.qidx[ek]
            self.qidx[ek] = (sub + 1) % self.NSUB
            ek = (ek, sub)
            if inc is None:
                inc = 16
        inc = inc if inc is not None else 1
        stream = self.streams[ek]
        need = {}

        def _add(semkey, val):
            if val > need.get(semkey, 0):
                need[semkey] = val

        for k in reads:
            lw = self.last_w.get(k)
            if lw:
                _add(*lw)
        for k in writes:
            lw = self.last_w.get(k)
            if lw:
                _add(*lw)
            for semkey, val in self.readers.get(k, {}).items():
                _add(semkey, val)
        waits = []
        for semkey in sorted(need, key=str):
            val = need[semkey]
            if semkey == "t" and stream == "tensor":
                continue
            if self.watermark.get((stream, semkey), 0) >= val:
                continue
            waits.append((self.sems[semkey], val))
            self.watermark[(stream, semkey)] = val
        myval = self.cnt[ek] + inc
        sem = self.sems[ek]
        self_wait = self.cnt[ek] if (isinstance(ek, tuple) and self.cnt[ek] > 0) else None

        def f(h):
            if self_wait is not None:
                h.wait_ge(sem, self_wait)
            for s, v in waits:
                h.wait_ge(s, v)
            inst = emit(h)
            inst.then_inc(sem, inc)

        self._emit(stream, f, is_dma=(ek in self.DMA_QUEUES))
        self.cnt[ek] = myval
        for k in writes:
            self.last_w[k] = (ek, myval)
            self.readers[k] = {}
        for k in reads:
            d = self.readers.setdefault(k, {})
            if myval > d.get(ek, 0):
                d[ek] = myval
        return myval

    def dma(self, ek, dst, src, reads=(), writes=(), **kw):
        return self.op(ek, lambda h: h.dma_start(dst, src, **kw),
                       reads=reads, writes=writes)

    def full_barrier(self):
        self.flush()
        snap = {k: self.cnt[k] for k in self.streams}
        for stream in sorted(set(self.streams.values())):
            def f(h, snap=snap):
                seen = {}
                for k, v in snap.items():
                    if v > 0:
                        hnd = self.sems[k]
                        prev = seen.get(id(hnd), (hnd, 0))[1]
                        seen[id(hnd)] = (hnd, max(v, prev))
                for hnd, v in seen.values():
                    h.wait_ge(hnd, v)
            getattr(self.block, stream)(f)
            for k in self.streams:
                self.watermark[(stream, k)] = max(
                    self.watermark.get((stream, k), 0), snap[k])

    def drain(self):
        todo = {}
        for k in self.streams:
            if self.cnt[k] > 0:
                h = self.sems[k]
                todo[id(h)] = (h, max(self.cnt[k], todo.get(id(h), (h, 0))[1]))

        self.flush()

        def f(h):
            for s, v in todo.values():
                h.wait_ge(s, v)
        self.block.sync(f)


class Common:
    pass


def alloc_common(stk, p, nc, cfg, names):
    c = Common()
    c.d = names
    c.ident_f = stk.enter_context(nc.sbuf_tensor([128, 128], F32))
    c.ident_b = stk.enter_context(nc.sbuf_tensor([128, 128], BF16))
    p.dma("qs", c.ident_f[:], names["ident_f"], writes=[("c", "idf")])
    p.dma("qs", c.ident_b[:], names["ident_b"], writes=[("c", "idb")])
    c.consts = stk.enter_context(nc.sbuf_tensor([128, 4], F32))
    p.op("v", lambda v: v.memset(c.consts[:, 0:1], 1e-5), writes=[("c", "k")])
    p.op("v", lambda v: v.memset(c.consts[:, 1:2], 1e-12), writes=[("c", "k")])
    c.sc_f32 = stk.enter_context(nc.sbuf_tensor([128, 1024], F32))
    c.st_small = stk.enter_context(nc.sbuf_tensor([128, 20], F32))
    c.x_sb = [stk.enter_context(nc.sbuf_tensor(uname(f"xsb{i}"), [128, 1024], F32)) for i in range(2)]
    c.xn_sb = c.x_sb
    return c


def emit_ln(p, c, slot, xkey, epscol=0):
    x = c.x_sb[slot]
    st = c.st_small
    kst = ("lnst",)
    p.op("v", lambda v: v.bn_stats(st[:, 0:6], x[:, 0:512]), reads=[xkey], writes=[kst])
    p.op("v", lambda v: v.bn_stats(st[:, 6:12], x[:, 512:1024]), reads=[xkey], writes=[kst])
    p.op("v", lambda v: v.bn_aggr(st[:, 12:14], st[:, 0:12]), reads=[kst], writes=[kst])
    p.op("s", lambda s: s.activation(st[:, 14:15], st[:, 13:14], AF.Ln,
                                     bias=c.consts[:, epscol:epscol + 1], scale=1.0),
         reads=[kst, ("c", "k")], writes=[kst])
    p.op("s", lambda s: s.activation(st[:, 15:16], st[:, 14:15], AF.Exp,
                                     bias=0.0, scale=-0.5),
         reads=[kst], writes=[kst])
    p.op("v", lambda v: v.tensor_scalar(st[:, 16:17], st[:, 12:13], -1.0, None, ALU.mult),
         reads=[kst], writes=[kst])
    p.op("v", lambda v: v.tensor_tensor(st[:, 17:18], st[:, 16:17], st[:, 15:16], ALU.mult),
         reads=[kst], writes=[kst])
    p.op("s", lambda s: s.activation(c.sc_f32[:], x[:], AF.Identity,
                                     bias=st[:, 17:18], scale=st[:, 15:16]),
         reads=[xkey, kst], writes=[("ysc",)])


def emit_transpose_f32(p, c, src_sb, src_key, ps, dst_emits):
    pkey = ("ps", "tr")

    def f(t):
        last = None
        for cc in range(8):
            last = t.transpose(ps[:, cc * 128:(cc + 1) * 128],
                               src_sb[:, cc * 128:(cc + 1) * 128], c.ident_f[:])
        return last
    p.op("t", f, reads=[src_key, ("c", "idf")], writes=[pkey])
    for ek, emitfn, rds, wrs in dst_emits:
        p.op(ek, emitfn, reads=list(rds) + [pkey], writes=list(wrs))


def emit_phase_l1ffn(p, stk, nc, c, cfg, xin, xout, xin_tag, xout_tag):
    d = c.d
    with ExitStack() as ph:
        win_sb = ph.enter_context(nc.sbuf_tensor([128, 8, 64], BF16))
        w1_sb = ph.enter_context(nc.sbuf_tensor([128, 16], F32))
        b1_sb = ph.enter_context(nc.sbuf_tensor([128, 16], F32))
        w2_sb = ph.enter_context(nc.sbuf_tensor([128, 16], F32))
        bin_sb = ph.enter_context(nc.sbuf_tensor([128, 1], F32))
        b2_sb = ph.enter_context(nc.sbuf_tensor([128, 1], F32))
        wout_sb = ph.enter_context(nc.sbuf_tensor([128, 1024], BF16))
        boutb = ph.enter_context(nc.sbuf_tensor([128, 1024], F32))
        yT = ph.enter_context(nc.sbuf_tensor([128, 8, 128], BF16))
        sT = ph.enter_context(nc.sbuf_tensor([128, 128], F32))
        tmp = ph.enter_context(nc.sbuf_tensor([128, 128], F32))
        gT = ph.enter_context(nc.sbuf_tensor([128, 128], F32))
        gB = ph.enter_context(nc.sbuf_tensor([128, 128], BF16))
        ps_t = ph.enter_context(nc.psum_tensor([128, 1024], F32))
        ps_s = ph.enter_context(nc.psum_tensor([128, 128], F32))
        ps_o = [ph.enter_context(nc.psum_tensor(uname(f"psffo{i}"), [128, 1024], F32)) for i in range(2)]

        p.dma("qs", win_sb[:], d["win"].rearrange("(db p) m -> p db m", p=128),
              writes=[("w", "win")])
        p.dma("qs", w1_sb[:], d["w1s"], writes=[("w", "w1")])
        p.dma("qs", b1_sb[:], d["b1s"], writes=[("w", "b1")])
        p.dma("qs", w2_sb[:], d["w2s"], writes=[("w", "w2")])
        p.dma("qs", bin_sb[:], d["bins"], writes=[("w", "bin")])
        p.dma("qs", b2_sb[:], d["b2s"], writes=[("w", "b2")])
        p.dma("qs", wout_sb[:], d["wouts"], writes=[("w", "wout")])
        p.dma("qs", boutb[:], bass.AP(d["bout"].tensor, 0, [[0, 128], [1, 1024]]),
              writes=[("w", "boutb")])
        p.op("v", lambda v: v.tensor_scalar(boutb[:], boutb[:], 0.5, None, ALU.mult),
             reads=[("w", "boutb")], writes=[("w", "boutb")])

        for pair in range(cfg.NT // 2):
            t0 = 2 * pair
            for j in range(2):
                t = t0 + j
                slot = t % 2
                p.dma("qs", c.x_sb[slot][:], xin[t * 128:(t + 1) * 128, :],
                      reads=[(xin_tag, t)], writes=[("xsb", slot)])
                emit_ln(p, c, slot, ("xsb", slot))

                def cp_bf(v):
                    return v.tensor_copy(
                        yT[:],
                        ps_t[:].rearrange("p (cc n) -> p cc n", cc=8))
                emit_transpose_f32(p, c, c.sc_f32, ("ysc",), ps_t,
                                   [("v", cp_bf, [], [("yT",)])])

                def mk_mm(jj):
                    def g(t_):
                        last = None
                        for db in range(8):
                            last = t_.matmul(
                                ps_s[jj * 64:(jj + 1) * 64, :],
                                win_sb[:, db, :],
                                yT[:, db, :],
                                start=(db == 0), stop=(db == 7),
                                tile_position=(0, jj * 64))
                        return last
                    return g
                p.op("t", mk_mm(j), reads=[("yT",), ("w", "win")],
                     writes=[("ps", "s", j)])
            p.op("v", lambda v: v.tensor_scalar(sT[:], ps_s[:], bin_sb[:],
                                                None, ALU.add),
                 reads=[("ps", "s", 0), ("ps", "s", 1), ("w", "bin")],
                 writes=[("sT",)])
            for i in range(16):
                p.op("v", lambda v, i=i: v.tensor_scalar(
                        tmp[:], sT[:], w1_sb[:, i:i + 1],
                        b1_sb[:, i:i + 1], ALU.mult, ALU.add),
                     reads=[("sT",), ("w", "w1"), ("w", "b1")], writes=[("tmp",)])
                p.op("v", lambda v: v.tensor_single_scalar(
                        tmp[:], tmp[:], 0.0, ALU.max),
                     reads=[("tmp",)], writes=[("tmp",)])
                if i == 0:
                    p.op("v", lambda v, i=i: v.tensor_scalar(
                            gT[:], tmp[:], w2_sb[:, i:i + 1], None, ALU.mult),
                         reads=[("tmp",), ("w", "w2")], writes=[("gT",)])
                else:
                    p.op("v", lambda v, i=i: v.scalar_tensor_tensor(
                            gT[:], tmp[:], w2_sb[:, i:i + 1],
                            gT[:], ALU.mult, ALU.add),
                         reads=[("tmp",), ("w", "w2"), ("gT",)], writes=[("gT",)])
            p.op("v", lambda v: v.tensor_scalar(
                    gB[:], gT[:], b2_sb[:], None, ALU.add),
                 reads=[("gT",), ("w", "b2")], writes=[("gB",)])
            for j in range(2):
                t = t0 + j
                slot = t % 2

                def mk_o(jj):
                    def g(t_):
                        last = None
                        for nch in range(2):
                            last = t_.matmul(
                                ps_o[jj][:, nch * 512:(nch + 1) * 512],
                                gB[jj * 64:(jj + 1) * 64, :],
                                wout_sb[jj * 64:(jj + 1) * 64,
                                        nch * 512:(nch + 1) * 512],
                                start=True, stop=True,
                                tile_position=(jj * 64, 0))
                        return last
                    return g
                p.op("t", mk_o(j), reads=[("gB",), ("w", "wout")],
                     writes=[("ps", "o", j)])
                p.op("v", lambda v, jj=j, sl=slot: v.scalar_tensor_tensor(
                        c.xn_sb[sl][:], ps_o[jj][:], 0.5, c.x_sb[sl][:],
                        ALU.mult, ALU.add),
                     reads=[("ps", "o", j), ("xsb", slot)], writes=[("xsb", slot)])
                p.op("v", lambda v, sl=slot: v.tensor_tensor(
                        c.xn_sb[sl][:], c.xn_sb[sl][:], boutb[:], ALU.add),
                     reads=[("xsb", slot), ("w", "boutb")], writes=[("xsb", slot)])
                p.dma("qa", xout[t * 128:(t + 1) * 128, :], c.xn_sb[slot][:],
                      reads=[("xsb", slot)], writes=[(xout_tag, t)])


def emit_phase_moe(p, stk, nc, c, cfg, xin, xout, xin_tag, xout_tag):
    d = c.d
    T, NT = cfg.T, cfg.NT
    NG = T // 512
    out_d = nc.dram_tensor(uname("moeoutd"), [T, 1024], BF16, kind="Internal").ap()
    with ExitStack() as ph:
        yT = ph.enter_context(nc.sbuf_tensor([128, 8, T], BF16))
        stg = ph.enter_context(nc.sbuf_tensor([128, 4, 2, 512], BF16))
        outt = ph.enter_context(nc.sbuf_tensor([128, 1024], BF16))
        mask_sb = ph.enter_context(nc.sbuf_tensor([128, NT, 8], F32))
        fcw = [ph.enter_context(nc.sbuf_tensor(uname(f"fcw{i}"), [128, 8, 2048], BF16))
               for i in range(2)]
        pjw = ph.enter_context(nc.sbuf_tensor([128, 16, 1024], BF16))
        hm = ph.enter_context(nc.sbuf_tensor([128, 16, 512], BF16))
        rt_sb = [ph.enter_context(nc.sbuf_tensor(uname(f"rtsb{i}"), [128, 512], BF16)) for i in range(3)]
        rwT_sb = ph.enter_context(nc.sbuf_tensor([128, 8, 8], F32))
        p.dma("qs", rwT_sb[:], d["rwT"].rearrange("(db p) e -> p db e", p=128),
              writes=[("w", "rwT")])

        with ExitStack() as pre:
            ytf = pre.enter_context(nc.sbuf_tensor([128, 1024], F32))
            lgt = pre.enter_context(nc.sbuf_tensor([128, 16], F32))
            ps_t = pre.enter_context(nc.psum_tensor([128, 1024], F32))
            ps_l = pre.enter_context(nc.psum_tensor([128, 8], F32))
            for t in range(NT):
                slot = t % 2
                p.dma("qs", c.x_sb[slot][:], xin[t * 128:(t + 1) * 128, :],
                      reads=[(xin_tag, t)], writes=[("xsb", slot)])
                emit_ln(p, c, slot, ("xsb", slot))

                def cp_f(v):
                    return v.tensor_copy(ytf[:], ps_t[:])

                def cp_b(s, tt=t):
                    return s.copy(yT[:, :, tt * 128:(tt + 1) * 128],
                                  ps_t[:].rearrange("p (cc n) -> p cc n", cc=8))
                emit_transpose_f32(p, c, c.sc_f32, ("ysc",), ps_t,
                                   [("v", cp_f, [], [("ytf",)]),
                                    ("s", cp_b, [], [("yT", t)])])

                def mk_lg(t_):
                    last = None
                    for db in range(8):
                        last = t_.matmul(ps_l[:], ytf[:, db * 128:(db + 1) * 128],
                                         rwT_sb[:, db, :],
                                         start=(db == 0), stop=(db == 7))
                    return last
                p.op("t", mk_lg, reads=[("ytf",), ("w", "rwT")], writes=[("ps", "l")])
                p.op("v", lambda v: v.tensor_copy(lgt[:, 0:8], ps_l[:]),
                     reads=[("ps", "l")], writes=[("lgt",)])
                p.op("v", lambda v: v.reduce_max(lgt[:, 8:9], lgt[:, 0:8], axis=AX.X),
                     reads=[("lgt",)], writes=[("lgt",)])
                p.op("v", lambda v, tt=t: v.tensor_scalar(
                        mask_sb[:, tt, :], lgt[:, 0:8], lgt[:, 8:9], None,
                        ALU.is_equal),
                     reads=[("lgt",)], writes=[("mask", t)])
            p.full_barrier()

        with ExitStack() as mn:
            ps_h = [mn.enter_context(nc.psum_tensor(uname(f"psh{i}"), [128, 512], F32)) for i in range(3)]
            ps_o = [mn.enter_context(nc.psum_tensor(uname(f"pso{i}"), [128, 512], F32)) for i in range(4)]
            for e in range(cfg.E):
                for half in range(2):
                    eh = e * 2 + half
                    fs = eh % 2
                    fsrc = d["fc"][e].rearrange("(db p) hc -> p db hc", p=128)[
                        :, :, half * 2048:(half + 1) * 2048]
                    psrc = d["pjT"][e].rearrange("(cc p) dd -> p cc dd", p=128)[
                        :, half * 16:(half + 1) * 16, :]
                    p.dma("qs", fcw[fs][:], fsrc, writes=[("fcw", fs)])
                    p.dma("qs", pjw[:], psrc, writes=[("pjw",)])
                    for g in range(NG):
                        ytkeys = [("yT", 4 * g + i) for i in range(4)]
                        for cc in range(16):
                            hs = cc % 3

                            def mk1(cc=cc, hs=hs, g=g, fs=fs):
                                def f(t_):
                                    last = None
                                    for db in range(8):
                                        last = t_.matmul(
                                            ps_h[hs][:],
                                            fcw[fs][:, db, cc * 128:(cc + 1) * 128],
                                            yT[:, db, g * 512:(g + 1) * 512],
                                            start=(db == 0), stop=(db == 7))
                                    return last
                                return f
                            p.op("t", mk1(), reads=[("fcw", fs)] + ytkeys,
                                 writes=[("ps", "h", hs)])
                            p.op("v", lambda v, hs=hs: v.tensor_single_scalar(
                                    rt_sb[hs][:], ps_h[hs][:], 0.0, ALU.max),
                                 reads=[("ps", "h", hs)], writes=[("rt", hs)])
                            p.op("g", lambda gg, hs=hs, cc=cc: gg.tensor_tensor(
                                    hm[:, cc, :], rt_sb[hs][:], rt_sb[hs][:],
                                    ALU.mult),
                                 reads=[("rt", hs)], writes=[("hm", cc)])
                        hmkeys = [("hm", cc2) for cc2 in range(16)]
                        for dh in range(2):
                            for lt in range(4):
                                def mk2(dh=dh, lt=lt):
                                    def f(t_):
                                        last = None
                                        for cc in range(16):
                                            last = t_.matmul(
                                                ps_o[lt][:],
                                                hm[:, cc, lt * 128:(lt + 1) * 128],
                                                pjw[:, cc, dh * 512:(dh + 1) * 512],
                                                start=(cc == 0), stop=(cc == 15))
                                        return last
                                    return f
                                p.op("t", mk2(), reads=hmkeys + [("pjw",)],
                                     writes=[("ps", "o", lt)])
                            for lt in range(4):
                                t = g * 4 + lt
                                p.op("v", lambda v, lt=lt, dh=dh, t=t, e=e:
                                     v.tensor_scalar(
                                        stg[:, lt, dh, :],
                                        ps_o[lt][:], mask_sb[:, t, e:e + 1],
                                        None, ALU.mult),
                                     reads=[("ps", "o", lt), ("mask", t)],
                                     writes=[("stg",)])
                        dst = out_d[g * 512:(g + 1) * 512, :].rearrange(
                            "(lt pp) (dh cc) -> pp lt dh cc", pp=128, dh=2)
                        kw = {} if (e == 0 and half == 0) else {"accum_op": ALU.add}
                        p.dma("qg", dst, stg[:],
                              reads=[("stg",)], writes=[("outd", g)], **kw)
            p.full_barrier()

        for t in range(NT):
            slot = t % 2
            p.dma("qs", c.x_sb[slot][:], xin[t * 128:(t + 1) * 128, :],
                  reads=[(xin_tag, t)], writes=[("xsb", slot)])
            p.dma("qs", outt[:], out_d[t * 128:(t + 1) * 128, :],
                  reads=[("outd", t // 4)], writes=[("outt",)])
            p.op("v", lambda v, t=t, sl=slot: v.scalar_tensor_tensor(
                    c.xn_sb[sl][:], outt[:], 0.5, c.x_sb[sl][:],
                    ALU.mult, ALU.add),
                 reads=[("outt",), ("xsb", slot)], writes=[("xsb", slot)])
            p.dma("qa", xout[t * 128:(t + 1) * 128, :], c.xn_sb[slot][:],
                  reads=[("xsb", slot)], writes=[(xout_tag, t)])


def emit_phase_l0attn(p, stk, nc, c, cfg, xin, xout, xin_tag, xout_tag):
    d = c.d
    S, T, NT, NS, Wp = cfg.S, cfg.T, cfg.NT, cfg.NS, cfg.Wp
    NRT = S // 128
    NLQ = S // 512
    NJ = Wp // 512
    qdr_d = [nc.dram_tensor(uname(f"qdrd{i}"), [S, Wp], BF16, kind="Internal").ap()
             for i in range(4)]
    kdx_d = [nc.dram_tensor(uname(f"kdxd{i}"), [S, Wp], BF16, kind="Internal").ap()
             for i in range(4)]
    ctx_d = nc.dram_tensor("ctxd", [cfg.Bc, 8, 128, S], BF16, kind="Internal").ap()
    ytmp_d = nc.dram_tensor("ytmpd", [T, 1024], BF16, kind="Internal").ap()
    zrow_d = nc.dram_tensor("zrowd", [1, S], F32, kind="Internal").ap()

    with ExitStack() as ph:
        yT = ph.enter_context(nc.sbuf_tensor([128, 8, T], BF16))
        ybf = ph.enter_context(nc.sbuf_tensor([128, 1024], BF16))
        with ExitStack() as pre:
            ps_t = pre.enter_context(nc.psum_tensor([128, 1024], F32))
            for t in range(NT):
                slot = t % 2
                p.dma("qs", c.x_sb[slot][:], xin[t * 128:(t + 1) * 128, :],
                      reads=[(xin_tag, t)], writes=[("xsb", slot)])
                emit_ln(p, c, slot, ("xsb", slot))
                p.op("v", lambda v: v.tensor_copy(ybf[:], c.sc_f32[:]),
                     reads=[("ysc",)], writes=[("ybf",)])
                p.dma("qa", ytmp_d[t * 128:(t + 1) * 128, :], ybf[:],
                      reads=[("ybf",)], writes=[("ytmp", t)])

                def cp_b(s, tt=t):
                    return s.copy(yT[:, :, tt * 128:(tt + 1) * 128],
                                  ps_t[:].rearrange("p (cc n) -> p cc n", cc=8))
                emit_transpose_f32(p, c, c.sc_f32, ("ysc",), ps_t,
                                   [("s", cp_b, [], [("yT", t)])])
            p.full_barrier()
        ytk = [("yT", t) for t in range(NT)]

        with ExitStack() as mn:
            ps_big = mn.enter_context(nc.psum_tensor([128, 2048], F32))
            ps_st = [mn.enter_context(nc.psum_tensor(uname(f"psst{i}"), [128, 1024], F32))
                     for i in range(2)]
            ps_u = ps_st[0]
            ps_bc = ps_st[1]
            pbq = [("psbig", q4) for q4 in range(4)]
            ex_t = mn.enter_context(nc.sbuf_tensor([128, Wp], BF16))
            exr_t = mn.enter_context(nc.sbuf_tensor([128, Wp], BF16))
            p.dma("qs", ex_t[:], d["extk"], writes=[("ext",)])
            p.dma("qs", exr_t[:], d["exrtk"], writes=[("exrt",)])
            wsl = [{nm: mn.enter_context(nc.sbuf_tensor(uname(f"wsl{i}_{nm}"),
                                                        [128, 8, 128], BF16))
                    for nm in ("q", "k", "v")} for i in range(2)]
            bqp_sb = mn.enter_context(nc.sbuf_tensor([128, 8], F32))
            bkp_sb = mn.enter_context(nc.sbuf_tensor([128, 8], F32))
            bvp_sb = mn.enter_context(nc.sbuf_tensor([128, 8], F32))
            p.dma("qs", bqp_sb[:], d["bqp"], writes=[("w", "bqp")])
            p.dma("qs", bkp_sb[:], d["bkp"], writes=[("w", "bkp")])
            p.dma("qs", bvp_sb[:], d["bvp"], writes=[("w", "bvp")])
            qT_p = [mn.enter_context(nc.sbuf_tensor(uname(f"qTp{i}"), [128, S], BF16))
                    for i in range(2)]
            kT_p = [mn.enter_context(nc.sbuf_tensor(uname(f"kTp{i}"), [128, S], BF16))
                    for i in range(2)]
            vau = [mn.enter_context(nc.sbuf_tensor(uname(f"vau{i}"), [128, 2, NRT, 65],
                                                   BF16)) for i in range(2)]
            rel = [mn.enter_context(nc.sbuf_tensor(uname(f"rel{i}"), [128, S], BF16))
                   for i in range(2)]
            qdsb = [mn.enter_context(nc.sbuf_tensor(uname(f"qdsb{i}"), [128, Wp], BF16))
                    for i in range(2)]
            squt = mn.enter_context(nc.sbuf_tensor([128, 512], BF16))
            stile = mn.enter_context(nc.sbuf_tensor([128, 2048], F32))
            ptile = mn.enter_context(nc.sbuf_tensor([128, 2048], BF16))
            ones_c = mn.enter_context(nc.sbuf_tensor([128, 1], BF16))
            ones_r = mn.enter_context(nc.sbuf_tensor([1, 128], BF16))
            usml = mn.enter_context(nc.sbuf_tensor([128, 16], F32))
            usmlb = mn.enter_context(nc.sbuf_tensor([128, 2], BF16))
            negu = mn.enter_context(nc.sbuf_tensor([128, 1], F32))
            zraw = mn.enter_context(nc.sbuf_tensor([128, S], F32))
            z0 = mn.enter_context(nc.sbuf_tensor([1, S], F32))
            z0b = mn.enter_context(nc.sbuf_tensor([1, S], BF16))
            ctxsb = mn.enter_context(nc.sbuf_tensor([64, S], BF16))
            ctxn = mn.enter_context(nc.sbuf_tensor([64, 512], BF16))
            p.op("v", lambda v: v.memset(ones_c[:], 1.0), writes=[("onec",)])
            p.dma("qs", ones_r[:], d["ones_row"], writes=[("oner",)])
            for i in range(2):
                p.op("v", lambda v, i=i: v.memset(vau[i][:], 0.0),
                     writes=[("vau", i, 0), ("vau", i, 1)])


            kex = ("emax",)
            p.op("v", lambda v: v.memset(usml[0:1, 0:1], 0.0), writes=[kex])
            for jc in range(NJ):
                p.op("g", lambda g, jc=jc: g.tensor_tensor(
                        squt[:], ex_t[:, jc * 512:(jc + 1) * 512],
                        ex_t[:, jc * 512:(jc + 1) * 512], ALU.mult),
                     reads=[("ext",)], writes=[("squt",)])
                p.op("t", lambda t_: t_.matmul(
                        ps_u[0:1, 0:512], ones_c[0:64, :], squt[0:64, :],
                        start=True, stop=True),
                     reads=[("squt",), ("onec",)], writes=[("ps", "st", 0)])
                p.op("v", lambda v: v.reduce_max(usml[0:1, 1:2], ps_u[0:1, 0:512],
                                                 axis=AX.X),
                     reads=[("ps", "st", 0)], writes=[("um1",)])
                p.op("v", lambda v: v.tensor_tensor(
                        usml[0:1, 0:1], usml[0:1, 0:1], usml[0:1, 1:2], ALU.max),
                     reads=[("um1",), kex], writes=[kex])
            p.op("s", lambda s: s.activation(usml[0:1, 3:4], usml[0:1, 0:1],
                                             AF.Ln),
                 reads=[kex], writes=[kex])
            p.op("s", lambda s: s.activation(usml[0:1, 2:3], usml[0:1, 3:4],
                                             AF.Exp, bias=0.0, scale=0.5),
                 reads=[kex], writes=[kex])

            def s1_proj(i):
                hp, b = divmod(i, cfg.Bc)
                par = i % 2
                ws = hp % 2
                if b == 0:
                    for nm, wname in (("q", "wq"), ("k", "wk"), ("v", "wv")):
                        srcw = d[wname][:, hp * 128:(hp + 1) * 128].rearrange(
                            "(eb p) dc -> p eb dc", p=128)
                        p.dma("qs", wsl[ws][nm][:], srcw, writes=[("wsl", ws, nm)])
                ts = b * S
                for nm, outbuf, okey, bkey, bias_sb in (
                        ("q", qT_p[par], ("qTp", par), ("w", "bqp"), bqp_sb),
                        ("k", kT_p[par], ("kTp", par), ("w", "bkp"), bkp_sb)):
                    for nchs in range(S // 512):
                        pp = nchs % 2

                        def mkp(nm=nm, nchs=nchs, pp=pp, ts=ts, ws=ws):
                            def f(t_):
                                last = None
                                for eb in range(8):
                                    last = t_.matmul(
                                        ps_st[pp][:, 0:512],
                                        wsl[ws][nm][:, eb, :],
                                        yT[:, eb, ts + nchs * 512:
                                           ts + (nchs + 1) * 512],
                                        start=(eb == 0), stop=(eb == 7))
                                return last
                            return f
                        p.op("t", mkp(), reads=[("wsl", ws, nm)] + ytk,
                             writes=[("ps", "st", pp)])
                        p.op("v", lambda v, outbuf=outbuf, nchs=nchs, pp=pp,
                             bias_sb=bias_sb, hp=hp: v.tensor_scalar(
                                outbuf[:, nchs * 512:(nchs + 1) * 512],
                                ps_st[pp][:, 0:512],
                                bias_sb[:, hp:hp + 1], None, ALU.add),
                             reads=[("ps", "st", pp), bkey],
                             writes=[okey])
                for lt in range(NRT):
                    pp = lt % 2

                    def mkv(lt=lt, pp=pp, ts=ts, ws=ws):
                        def f(t_):
                            last = None
                            for eb in range(8):
                                last = t_.matmul(
                                    ps_st[pp][:, 0:128],
                                    yT[:, eb, ts + lt * 128:ts + (lt + 1) * 128],
                                    wsl[ws]["v"][:, eb, :],
                                    start=(eb == 0), stop=(eb == 7))
                            return last
                        return f
                    p.op("t", mkv(), reads=[("wsl", ws, "v")] + ytk,
                         writes=[("ps", "st", pp)])
                    for h2 in range(2):
                        p.op("v", lambda v, lt=lt, pp=pp, h2=h2, par=par:
                             v.tensor_copy(
                                vau[par][:, h2, lt, 0:64],
                                ps_st[pp][0:128, h2 * 64:(h2 + 1) * 64]),
                             reads=[("ps", "st", pp)], writes=[("vau", par, h2)])
                for h2 in range(2):
                    p.op("v", lambda v, h2=h2, par=par: v.memset(
                            vau[par][:, h2, :, 64:65], 1.0),
                         writes=[("vau", par, h2)])

            def s2_tables(i):
                par = i % 2
                for h2 in range(2):
                    tslot = 2 * par + h2
                    for tabname, srcT, srckey, srcbuf, sbkey, dstd, dkey, qq in (
                            ("qdr", exr_t, ("exrt",), qT_p[par], ("qTp", par),
                             qdr_d, "qd", "qg"),
                            ("kdx", ex_t, ("ext",), kT_p[par], ("kTp", par),
                             kdx_d, "kd", "qs")):
                        for lt in range(NRT):
                            for jc in range(NJ):
                                qb = jc % 4

                                def mkt(h2=h2, lt=lt, jc=jc, qb=qb,
                                        srcbuf=srcbuf, srcT=srcT):
                                    def f(t_):
                                        return t_.matmul(
                                            ps_big[:, qb * 512:(qb + 1) * 512],
                                            srcbuf[h2 * 64:(h2 + 1) * 64,
                                                   lt * 128:(lt + 1) * 128],
                                            srcT[h2 * 64:(h2 + 1) * 64,
                                                 jc * 512:(jc + 1) * 512],
                                            start=True, stop=True)
                                    return f
                                p.op("t", mkt(), reads=[sbkey, srckey],
                                     writes=[("psbig", qb)])

                                def mkev(jc=jc, qb=qb, par=par):
                                    def f(v):
                                        return v.tensor_copy(
                                            qdsb[par][:, jc * 512:(jc + 1) * 512],
                                            ps_big[:, qb * 512:(qb + 1) * 512])
                                    return f
                                p.op("v", mkev(), reads=[("psbig", qb)],
                                     writes=[("qdsb", par, jc % 2)])
                            p.dma(qq, dstd[tslot][lt * 128:(lt + 1) * 128, :],
                                  qdsb[par][:],
                                  reads=[("qdsb", par, 0), ("qdsb", par, 1)],
                                  writes=[(dkey, tslot, lt)])

            def s3_sweeps(i):
                hp, b = divmod(i, cfg.Bc)
                par = i % 2
                for h2 in range(2):
                    emit_l0_head_sweep(
                        p, c, cfg, b, hp, h2, 2 * par + h2, qdr_d, kdx_d,
                        zrow_d, qT_p[par], kT_p[par], vau[par], rel, stile,
                        ptile, squt, ones_c, ones_r, usml, usmlb, negu, zraw,
                        z0, z0b, ctxsb, ctxn, ctx_d, bvp_sb,
                        ps_big, ps_st, ps_u, ps_bc, pbq, par)

            npair = 8 * cfg.Bc
            s1_proj(0)
            s2_tables(0)
            for i in range(1, npair):
                s1_proj(i)
                s2_tables(i)
                s3_sweeps(i - 1)
            s3_sweeps(npair - 1)
            p.full_barrier()

        with ExitStack() as po:
            wo_sb = po.enter_context(nc.sbuf_tensor([128, 8, 1024], BF16))
            bo_b = po.enter_context(nc.sbuf_tensor([128, 1024], F32))
            ga_b = po.enter_context(nc.sbuf_tensor([128, 1024], F32))
            be_b = po.enter_context(nc.sbuf_tensor([128, 1024], F32))
            ctxl = po.enter_context(nc.sbuf_tensor([128, 8, 128], BF16))
            hsb = po.enter_context(nc.sbuf_tensor([128, 1024], F32))
            ps_po = [po.enter_context(nc.psum_tensor(uname(f"pspo{i}"), [128, 1024], F32))
                     for i in range(2)]
            p.dma("qs", wo_sb[:], d["wo"].rearrange("(db p) e -> p db e", p=128),
                  writes=[("w", "wo")])
            p.dma("qs", bo_b[:], bass.AP(d["bo"].tensor, 0, [[0, 128], [1, 1024]]),
                  writes=[("w", "bo")])
            p.dma("qs", ga_b[:], bass.AP(d["lng"].tensor, 0, [[0, 128], [1, 1024]]),
                  writes=[("w", "ga")])
            p.dma("qs", be_b[:], bass.AP(d["lnb"].tensor, 0, [[0, 128], [1, 1024]]),
                  writes=[("w", "be")])
            for t in range(NT):
                b, lt = t // NS, t % NS
                slot = t % 2
                pp = t % 2
                p.dma("qs", ctxl[:],
                      ctx_d[b, :, :, lt * 128:(lt + 1) * 128]
                      .rearrange("hp p l -> p hp l"),
                      reads=[("ctxd", b, hp2, h22, lt) for hp2 in range(8)
                             for h22 in range(2)],
                      writes=[("ctxl",)])
                p.dma("qs", c.x_sb[slot][:], xin[t * 128:(t + 1) * 128, :],
                      reads=[(xin_tag, t)], writes=[("xsb", slot)])
                p.dma("qs", ybf[:], ytmp_d[t * 128:(t + 1) * 128, :],
                      reads=[("ytmp", t)], writes=[("ybf",)])

                def mko(pp=pp):
                    def f(t_):
                        last = None
                        for nch in range(2):
                            for hp2 in range(8):
                                last = t_.matmul(
                                    ps_po[pp][:, nch * 512:(nch + 1) * 512],
                                    ctxl[:, hp2, :],
                                    wo_sb[:, hp2, nch * 512:(nch + 1) * 512],
                                    start=(hp2 == 0), stop=(hp2 == 7))
                        return last
                    return f
                p.op("t", mko(), reads=[("ctxl",), ("w", "wo")],
                     writes=[("ps", "po", pp)])
                p.op("v", lambda v, pp=pp: v.tensor_tensor(
                        hsb[:], ps_po[pp][:], bo_b[:], ALU.add),
                     reads=[("ps", "po", pp), ("w", "bo")], writes=[("hsb",)])
                p.op("v", lambda v: v.tensor_tensor(hsb[:], hsb[:], ybf[:], ALU.add),
                     reads=[("hsb",), ("ybf",)], writes=[("hsb",)])
                emit_ln_generic(p, c, hsb, ("hsb",), epscol=1)
                p.op("v", lambda v: v.tensor_tensor(c.sc_f32[:], c.sc_f32[:],
                                                    ga_b[:], ALU.mult),
                     reads=[("ysc",), ("w", "ga")], writes=[("ysc",)])
                p.op("v", lambda v: v.tensor_tensor(c.sc_f32[:], c.sc_f32[:],
                                                    be_b[:], ALU.add),
                     reads=[("ysc",), ("w", "be")], writes=[("ysc",)])
                p.op("v", lambda v, sl=slot: v.tensor_tensor(
                        c.xn_sb[sl][:], c.sc_f32[:], c.x_sb[sl][:], ALU.add),
                     reads=[("ysc",), ("xsb", slot)], writes=[("xsb", slot)])
                p.dma("qa", xout[t * 128:(t + 1) * 128, :], c.xn_sb[slot][:],
                      reads=[("xsb", slot)], writes=[(xout_tag, t)])
            p.full_barrier()


def emit_l0_head_sweep(p, c, cfg, b, hp, h2, tslot, qdr_d, kdx_d, zrow_d,
                       qT_p, kT_p, vau, rel, stile, ptile, squt,
                       ones_c, ones_r, usml, usmlb, negu, zraw, z0, z0b,
                       ctxsb, ctxn, ctx_d, bvp_sb,
                       ps_big, ps_st, ps_u, ps_bc, pbq, par=0):
    S, Wp = cfg.S, cfg.Wp
    NRT, NLQ = S // 128, S // 512
    r0h, r1h = h2 * 64, (h2 + 1) * 64

    for srcbuf, sbkey, col in ((qT_p, ("qTp", par), 4), (kT_p, ("kTp", par), 5)):
        p.op("v", lambda v, col=col: v.memset(usml[0:1, col:col + 1], 0.0),
             writes=[("umax", col)])
        for ch in range(S // 512):
            p.op("g", lambda g, ch=ch, srcbuf=srcbuf: g.tensor_tensor(
                    squt[:], srcbuf[:, ch * 512:(ch + 1) * 512],
                    srcbuf[:, ch * 512:(ch + 1) * 512], ALU.mult),
                 reads=[sbkey], writes=[("squt",)])
            p.op("t", lambda t_: t_.matmul(
                    ps_u[0:1, 0:512], ones_c[r0h:r1h, :], squt[r0h:r1h, :],
                    start=True, stop=True),
                 reads=[("squt",), ("onec",)], writes=[("ps", "st", 0)])
            p.op("v", lambda v: v.reduce_max(usml[0:1, 8:9], ps_u[0:1, 0:512],
                                             axis=AX.X),
                 reads=[("ps", "st", 0)], writes=[("um8",)])
            p.op("v", lambda v, col=col: v.tensor_tensor(
                    usml[0:1, col:col + 1], usml[0:1, col:col + 1],
                    usml[0:1, 8:9], ALU.max),
                 reads=[("um8",), ("umax", col)], writes=[("umax", col)])
        p.op("s", lambda s, col=col: s.activation(
                usml[0:1, 12:13], usml[0:1, col:col + 1], AF.Ln),
             reads=[("umax", col)], writes=[("umax", col), ("u12",)])
        p.op("s", lambda s, col=col: s.activation(
                usml[0:1, col + 2:col + 3], usml[0:1, 12:13], AF.Exp,
                bias=0.0, scale=0.5),
             reads=[("umax", col), ("u12",)], writes=[("umax", col)])
    p.op("v", lambda v: v.tensor_tensor(usml[0:1, 9:10], usml[0:1, 6:7],
                                        usml[0:1, 2:3], ALU.add),
         reads=[("umax", 4), ("emax",)], writes=[("usc",)])
    p.op("v", lambda v: v.tensor_tensor(usml[0:1, 10:11], usml[0:1, 7:8],
                                        usml[0:1, 2:3], ALU.add),
         reads=[("umax", 5), ("emax",), ("usc",)], writes=[("usc",)])
    p.op("v", lambda v: v.tensor_tensor(usml[0:1, 11:12], usml[0:1, 9:10],
                                        usml[0:1, 10:11], ALU.mult),
         reads=[("usc",)], writes=[("usc",)])
    p.op("v", lambda v: v.tensor_scalar(usmlb[0:1, 0:1], usml[0:1, 11:12],
                                        -0.125, None, ALU.mult),
         reads=[("usc",)], writes=[("usc",)])
    p.op("t", lambda t_: t_.matmul(ps_u[0:128, 0:1], ones_r[0:1, :],
                                   usmlb[0:1, 0:1], start=True, stop=True),
         reads=[("usc",), ("oner",)], writes=[("ps", "st", 0)])
    p.op("v", lambda v: v.tensor_copy(negu[:], ps_u[:, 0:1]),
         reads=[("ps", "st", 0)], writes=[("negu",)])

    def rel_fetch(rt):
        slot = rt % 2
        r0 = rt * 128
        qsrc = bass.AP(qdr_d[tslot].tensor, (S - 1) + r0,
                       [[Wp - 1, S], [1, 128]])
        p.dma("qs", rel[slot][:], qsrc, transpose=True,
              reads=[("qd", tslot, lt2) for lt2 in range(NRT)],
              writes=[("rel", slot)])
        ksrc = bass.AP(kdx_d[tslot].tensor, (S - 1) + r0 * (Wp - 1),
                       [[Wp - 1, 128], [1, S]])
        p.dma("qg", rel[slot][:], ksrc, accum_op=ALU.add,
              reads=[("kd", tslot, lt2) for lt2 in range(NRT)] + [("rel", slot)],
              writes=[("rel", slot)])

    rel_fetch(0)
    nhb = (NLQ + 1) // 2
    for rt in range(NRT):
        slot = rt % 2
        if rt + 1 < NRT:
            rel_fetch(rt + 1)
        for hb in range(nhb):
            lqs = [lq for lq in (2 * hb, 2 * hb + 1) if lq < NLQ]
            w = len(lqs) * 512

            def mkst(rt=rt, hb=hb, lqs=lqs):
                def f(t_):
                    last = None
                    for i, lq in enumerate(lqs):
                        last = t_.matmul(
                            ps_st[hb][:, i * 512:(i + 1) * 512],
                            kT_p[r0h:r1h, rt * 128:(rt + 1) * 128],
                            qT_p[r0h:r1h, lq * 512:(lq + 1) * 512],
                            start=True, stop=True)
                    return last
                return f
            p.op("t", mkst(), reads=[("kTp", par), ("qTp", par)],
                 writes=[("ps", "st", hb)])
            p.op("v", lambda v, hb=hb, slot=slot, w=w: v.tensor_tensor(
                    stile[:, hb * 1024:hb * 1024 + w], ps_st[hb][:, 0:w],
                    rel[slot][:, hb * 1024:hb * 1024 + w], ALU.add),
                 reads=[("ps", "st", hb), ("rel", slot)], writes=[("stile", hb)])
        p.op("s", lambda s: s.activation(
                ptile[:, 0:S], stile[:, 0:S], AF.Exp,
                bias=negu[:], scale=0.125),
             reads=[("stile", 0), ("stile", 1), ("negu",)], writes=[("pt",)])

        def mkpv(rt=rt):
            def f(t_):
                last = None
                for lq in range(NLQ):
                    last = t_.matmul(
                        ps_big[0:65, lq * 512:(lq + 1) * 512],
                        vau[:, h2, rt, :],
                        ptile[:, lq * 512:(lq + 1) * 512],
                        start=(rt == 0), stop=(rt == NRT - 1))
                return last
            return f
        p.op("t", mkpv(), reads=[("pt",), ("vau", par, h2)],
             writes=[pbq[lq2] for lq2 in range(NLQ)])
    p.op("v", lambda v: v.tensor_copy(zraw[64:65, 0:S], ps_big[64:65, 0:S]),
         reads=pbq[:NLQ], writes=[("zraw",)])
    p.dma("qa", zrow_d[:], zraw[64:65, 0:S], reads=[("zraw",)], writes=[("zrd",)])
    p.dma("qa", z0[:], zrow_d[:], reads=[("zrd",)], writes=[("z0",)])
    p.op("v", lambda v: v.reciprocal(z0[:], z0[:]), reads=[("z0",)],
         writes=[("z0",)])
    p.op("v", lambda v: v.tensor_copy(z0b[:], z0[:]), reads=[("z0",)],
         writes=[("z0b",)])
    p.op("v", lambda v: v.tensor_copy(ctxsb[:, 0:S], ps_big[0:64, 0:S]),
         reads=pbq[:NLQ], writes=[("ctxsb",)])
    for ch in range(S // 512):
        p.op("t", lambda t_, ch=ch: t_.matmul(
                ps_bc[0:64, 0:512], ones_r[0:1, 0:64],
                z0b[0:1, ch * 512:(ch + 1) * 512], start=True, stop=True),
             reads=[("z0b",), ("oner",)], writes=[("ps", "st", 1)])
        p.op("v", lambda v, ch=ch: v.tensor_tensor(
                ctxn[:], ctxsb[:, ch * 512:(ch + 1) * 512], ps_bc[0:64, 0:512],
                ALU.mult),
             reads=[("ps", "st", 1), ("ctxsb",)], writes=[("ctxn",)])
        p.op("v", lambda v: v.tensor_scalar(
                ctxn[:], ctxn[:], bvp_sb[r0h:r1h, hp:hp + 1], None, ALU.add),
             reads=[("ctxn",), ("w", "bvp")], writes=[("ctxn",)])
        p.dma("qa", ctx_d[b, hp, r0h:r1h, ch * 512:(ch + 1) * 512],
              ctxn[:], reads=[("ctxn",)],
              writes=[("ctxd", b, hp, h2, lt2) for lt2 in
                      range(ch * 4, ch * 4 + 4)])


def emit_ln_generic(p, c, src_sb, xkey, epscol=0):
    st = c.st_small
    kst = ("lnst",)
    p.op("v", lambda v: v.bn_stats(st[:, 0:6], src_sb[:, 0:512]),
         reads=[xkey], writes=[kst])
    p.op("v", lambda v: v.bn_stats(st[:, 6:12], src_sb[:, 512:1024]),
         reads=[xkey], writes=[kst])
    p.op("v", lambda v: v.bn_aggr(st[:, 12:14], st[:, 0:12]), reads=[kst], writes=[kst])
    p.op("s", lambda s: s.activation(st[:, 14:15], st[:, 13:14], AF.Ln,
                                     bias=c.consts[:, epscol:epscol + 1], scale=1.0),
         reads=[kst, ("c", "k")], writes=[kst])
    p.op("s", lambda s: s.activation(st[:, 15:16], st[:, 14:15], AF.Exp,
                                     bias=0.0, scale=-0.5),
         reads=[kst], writes=[kst])
    p.op("v", lambda v: v.tensor_scalar(st[:, 16:17], st[:, 12:13], -1.0, None, ALU.mult),
         reads=[kst], writes=[kst])
    p.op("v", lambda v: v.tensor_tensor(st[:, 17:18], st[:, 16:17], st[:, 15:16], ALU.mult),
         reads=[kst], writes=[kst])
    p.op("s", lambda s: s.activation(c.sc_f32[:], src_sb[:], AF.Identity,
                                     bias=st[:, 17:18], scale=st[:, 15:16]),
         reads=[xkey, kst], writes=[("ysc",)])


def emit_phase_l1attn(p, stk, nc, c, cfg, xin, xout, xin_tag, xout_tag):
    d = c.d
    S, T, NT, NS = cfg.S, cfg.T, cfg.NT, cfg.NS
    LB, TK = cfg.LB, cfg.TK
    sband_d = [nc.dram_tensor(f"sbandd{i}", [128, 3 * 128], F32, kind="Internal").ap()
               for i in range(2)]
    w2r_d = [nc.dram_tensor(f"w2rd{i}", [128, 2 * LB], BF16, kind="Internal").ap()
             for i in range(2)]
    phr_d = [nc.dram_tensor(f"phrd{i}", [1, 128], F32, kind="Internal").ap()
             for i in range(2)]

    with ExitStack() as ph:
        wsb = {}
        for nm in ("q", "k", "pk", "v", "pv"):
            wsb[nm] = ph.enter_context(
                nc.sbuf_tensor(uname(f"l1w_{nm}"), [128, 8, 1024], BF16))
            p.dma("qs", wsb[nm][:],
                  d[nm + "wT"].rearrange("(jb p) i -> p jb i", p=128),
                  writes=[("w", nm)])
        qbp_sb = ph.enter_context(nc.sbuf_tensor([128, 8], F32))
        kbp_sb = ph.enter_context(nc.sbuf_tensor([128, 8], F32))
        pkbp_sb = ph.enter_context(nc.sbuf_tensor([128, 8], F32))
        p.dma("qs", qbp_sb[:], d["qbp"], writes=[("w", "qbp")])
        p.dma("qs", kbp_sb[:], d["kbp"], writes=[("w", "kbp")])
        p.dma("qs", pkbp_sb[:], d["pkbp"], writes=[("w", "pkbp")])
        vb_b = ph.enter_context(nc.sbuf_tensor([128, 1024], F32))
        pvb_b = ph.enter_context(nc.sbuf_tensor([128, 1024], F32))
        p.dma("qs", vb_b[:], bass.AP(d["vbr"].tensor, 0, [[0, 128], [1, 1024]]),
              writes=[("w", "vb")])
        p.dma("qs", pvb_b[:], bass.AP(d["pvbr"].tensor, 0, [[0, 128], [1, 1024]]),
              writes=[("w", "pvb")])
        lm_sb = ph.enter_context(nc.sbuf_tensor([128, 128], BF16))
        p.dma("qs", lm_sb[:], d["lmatT"], writes=[("w", "lm")])
        vm_sb = [ph.enter_context(nc.sbuf_tensor(uname(f"vmsb{i}"), [128, LB], F32))
                 for i in range(2)]
        p.dma("qs", vm_sb[0][:], d["vm0"], writes=[("w", "vm0")])
        p.dma("qs", vm_sb[1][:], d["vm1"], writes=[("w", "vm1")])
        vmn_sb = [ph.enter_context(nc.sbuf_tensor(uname(f"vmn{i}"), [128, LB], F32))
                  for i in range(2)]
        p.dma("qs", vmn_sb[0][:], d["vmn0"], writes=[("w", "vmn0")])
        p.dma("qs", vmn_sb[1][:], d["vmn1"], writes=[("w", "vmn1")])
        zpad = ph.enter_context(nc.sbuf_tensor([128, 2 * LB], BF16))
        p.op("v", lambda v: v.memset(zpad[:], 0.0), writes=[("zpad",)])
        for i in range(2):
            p.dma("qa", w2r_d[i][:], zpad[:], reads=[("zpad",)],
                  writes=[("w2rd", i)])

        ytile = ph.enter_context(nc.sbuf_tensor([128, 1024], BF16))
        yT = ph.enter_context(nc.sbuf_tensor([128, 8, 128], BF16))
        phf = ph.enter_context(nc.sbuf_tensor([128, 1024], F32))
        phT = ph.enter_context(nc.sbuf_tensor([128, 8, 128], BF16))
        qT_t = ph.enter_context(nc.sbuf_tensor([128, 8, 128], BF16))
        kT_r = [ph.enter_context(nc.sbuf_tensor(uname(f"ktr{i}"), [128, 8, 128], BF16))
                for i in range(4)]
        v_r = [ph.enter_context(nc.sbuf_tensor(uname(f"vr{i}"), [128, 1024], BF16))
               for i in range(4)]
        pkT_r = [ph.enter_context(nc.sbuf_tensor(uname(f"pktr{i}"), [128, 8, 128], BF16))
                 for i in range(3)]
        pv_r = [ph.enter_context(nc.sbuf_tensor(uname(f"pvr{i}"), [128, 1024], BF16))
                for i in range(3)]
        for i in range(4):
            p.op("v", lambda v, i=i: v.memset(kT_r[i][:], 0.0), writes=[("ktr", i)])
            p.op("v", lambda v, i=i: v.memset(v_r[i][:], 0.0), writes=[("vr", i)])
        for i in range(3):
            p.op("v", lambda v, i=i: v.memset(pkT_r[i][:], 0.0), writes=[("pktr", i)])
            p.op("v", lambda v, i=i: v.memset(pv_r[i][:], 0.0), writes=[("pvr", i)])
        pkprev = ph.enter_context(nc.sbuf_tensor([128, 8, 128], BF16))
        pvprev = ph.enter_context(nc.sbuf_tensor([128, 1024], BF16))
        tq = ph.enter_context(nc.sbuf_tensor([128, 1024], BF16))
        sband_sb = ph.enter_context(nc.sbuf_tensor([128, 3 * 128], F32))
        s2r = ph.enter_context(nc.sbuf_tensor([128, LB], F32))
        wrk = ph.enter_context(nc.sbuf_tensor([128, LB], F32))
        expv = ph.enter_context(nc.sbuf_tensor([128, LB], F32))
        em = ph.enter_context(nc.sbuf_tensor([128, LB], F32))
        w2r = ph.enter_context(nc.sbuf_tensor([128, LB], BF16))
        wsk = ph.enter_context(nc.sbuf_tensor([128, 3 * 128], BF16))
        wT_sb = ph.enter_context(nc.sbuf_tensor([128, 3, 128], BF16))
        phrr = [ph.enter_context(nc.sbuf_tensor(uname(f"phrr{i}"), [1, 128], F32))
                for i in range(2)]
        phrc = [ph.enter_context(nc.sbuf_tensor(uname(f"phrc{i}"), [128, 1], F32))
                for i in range(2)]
        st2 = ph.enter_context(nc.sbuf_tensor([128, 8], F32))
        ones_c = ph.enter_context(nc.sbuf_tensor([128, 1], BF16))
        p.op("v", lambda v: v.memset(ones_c[:], 1.0), writes=[("onec",)])
        mr8 = ph.enter_context(nc.sbuf_tensor([128, 8], F32))
        p.op("v", lambda v: v.memset(mr8[:], 3.0e38), writes=[("mr8",)])
        out_sb = ph.enter_context(nc.sbuf_tensor([128, 1024], F32))

        with ExitStack() as mn:
            ps_t = mn.enter_context(nc.psum_tensor([128, 1024], F32))
            ps_band = mn.enter_context(nc.psum_tensor([128, 3 * 128], F32))
            ps_pp = [mn.enter_context(nc.psum_tensor(uname(f"l1pp{i}"), [128, 512], F32))
                     for i in range(2)]
            ps_o = mn.enter_context(nc.psum_tensor([128, 1024], F32))
            ps_wt = mn.enter_context(nc.psum_tensor([128, 3 * 128], BF16))

            def l1_stageA(gt):
                sq, t = divmod(gt, NS)
                slot = gt % 2
                kcur, kp1, kp2 = gt % 4, (gt - 1) % 4, (gt - 2) % 4
                pcur, pprev = gt % 3, (gt - 1) % 3
                p.dma("qs", c.x_sb[slot][:], xin[gt * 128:(gt + 1) * 128, :],
                      reads=[(xin_tag, gt)], writes=[("xsb", slot)])
                emit_ln(p, c, slot, ("xsb", slot))
                p.op("v", lambda v: v.tensor_copy(ytile[:], c.sc_f32[:]),
                     reads=[("ysc",)], writes=[("ytile",)])

                def cp_b(s):
                    return s.copy(yT[:],
                                  ps_t[:].rearrange("p (cc n) -> p cc n", cc=8))
                emit_transpose_f32(p, c, c.sc_f32, ("ysc",), ps_t,
                                   [("s", cp_b, [], [("yTt",)])])
                p.op("t", lambda t_: [
                        t_.matmul(ps_t[:, 0:512], lm_sb[:], ytile[:, 0:512],
                                  start=True, stop=True),
                        t_.matmul(ps_t[:, 512:1024], lm_sb[:],
                                  ytile[:, 512:1024], start=True, stop=True)][-1],
                     reads=[("w", "lm"), ("ytile",)], writes=[("ps", "tr")])
                p.op("v", lambda v: v.tensor_copy(phf[:], ps_t[:]),
                     reads=[("ps", "tr")], writes=[("phf",)])

                def cp_ph(s):
                    return s.copy(phT[:],
                                  ps_t[:].rearrange("p (cc n) -> p cc n", cc=8))
                emit_transpose_f32(p, c, phf, ("phf",), ps_t,
                                   [("s", cp_ph, [], [("phT",)])])
                for nm, outbuf, okey, bp_sb, bpkey, rhsbuf, rhskey in (
                        ("q", qT_t, ("qTt",), qbp_sb, ("w", "qbp"), yT, ("yTt",)),
                        ("k", kT_r[kcur], ("ktr", kcur), kbp_sb, ("w", "kbp"),
                         yT, ("yTt",)),
                        ("pk", pkT_r[pcur], ("pktr", pcur), pkbp_sb,
                         ("w", "pkbp"), phT, ("phT",))):
                    for ib in range(8):
                        pp = ib % 2

                        def mkq(nm=nm, ib=ib, pp=pp, rhsbuf=rhsbuf):
                            def f(t_):
                                last = None
                                for jb in range(8):
                                    last = t_.matmul(
                                        ps_pp[pp][:, 0:128],
                                        wsb[nm][:, jb, ib * 128:(ib + 1) * 128],
                                        rhsbuf[:, jb, :],
                                        start=(jb == 0), stop=(jb == 7))
                                return last
                            return f
                        p.op("t", mkq(), reads=[("w", nm), rhskey],
                             writes=[("ps", "pp", pp)])
                        p.op("v", lambda v, outbuf=outbuf, ib=ib, pp=pp,
                             bp_sb=bp_sb: v.tensor_scalar(
                                outbuf[:, ib, :], ps_pp[pp][:, 0:128],
                                bp_sb[:, ib:ib + 1], None, ALU.add),
                             reads=[("ps", "pp", pp), bpkey], writes=[okey])
                for nm, ring, rkey, bb, bbkey, lhsbuf, lhskey in (
                        ("v", v_r[kcur], ("vr", kcur), vb_b, ("w", "vb"),
                         yT, ("yTt",)),
                        ("pv", pv_r[pcur], ("pvr", pcur), pvb_b, ("w", "pvb"),
                         phT, ("phT",))):
                    def mkvm(nm=nm, lhsbuf=lhsbuf):
                        def f(t_):
                            last = None
                            for nch in range(2):
                                for jb in range(8):
                                    last = t_.matmul(
                                        ps_o[:, nch * 512:(nch + 1) * 512],
                                        lhsbuf[:, jb, :],
                                        wsb[nm][:, jb, nch * 512:(nch + 1) * 512],
                                        start=(jb == 0), stop=(jb == 7))
                            return last
                        return f
                    p.op("t", mkvm(), reads=[("w", nm), lhskey],
                         writes=[("ps", "o")])
                    p.op("v", lambda v, ring=ring, bb=bb: v.tensor_tensor(
                            ring[:], ps_o[:], bb[:], ALU.add),
                         reads=[("ps", "o"), bbkey], writes=[rkey])
                def mkband(t=t, kcur=kcur, kp1=kp1, kp2=kp2):
                    ring_idx = [kp2, kp1, kcur]

                    def f(t_):
                        last = None
                        for w in range(3):
                            for jb in range(8):
                                last = t_.matmul(
                                    ps_band[:, w * 128:(w + 1) * 128],
                                    qT_t[:, jb, :],
                                    kT_r[ring_idx[w]][:, jb, :],
                                    start=(jb == 0), stop=(jb == 7))
                        return last
                    return f
                p.op("t", mkband(),
                     reads=[("qTt",), ("ktr", kcur), ("ktr", kp1), ("ktr", kp2)],
                     writes=[("ps", "band")])
                p.op("v", lambda v: v.tensor_copy(sband_sb[:], ps_band[:]),
                     reads=[("ps", "band")], writes=[("sband",)])
                p.dma("qa", sband_d[slot][:], sband_sb[:],
                      reads=[("sband",)], writes=[("sbandd", slot)])
                p.op("v", lambda v, pcur=pcur: v.tensor_copy(
                        pkprev[:, :, 1:128], pkT_r[pcur][:, :, 0:127]),
                     reads=[("pktr", pcur)], writes=[("pkprev",)])
                if t == 0:
                    p.op("v", lambda v: v.memset(pkprev[:, :, 0:1], 0.0),
                         reads=[], writes=[("pkprev",)])
                else:
                    p.op("v", lambda v, pprev=pprev: v.tensor_copy(
                            pkprev[:, :, 0:1], pkT_r[pprev][:, :, 127:128]),
                         reads=[("pktr", pprev)], writes=[("pkprev",)])
                p.op("v", lambda v: v.tensor_tensor(tq[:],
                                                    qT_t[:].rearrange(
                                                        "p a b -> p (a b)"),
                                                    pkprev[:].rearrange(
                                                        "p a b -> p (a b)"),
                                                    ALU.mult),
                     reads=[("qTt",), ("pkprev",)], writes=[("tq",)])

                def mkphr(t_):
                    last = None
                    for jb in range(8):
                        last = t_.matmul(ps_band[0:1, 0:128], ones_c[:],
                                         tq[:, jb * 128:(jb + 1) * 128],
                                         start=(jb == 0), stop=(jb == 7))
                    return last
                p.op("t", mkphr, reads=[("tq",), ("onec",)],
                     writes=[("ps", "band")])
                p.op("v", lambda v: v.tensor_copy(phrr[slot][:], ps_band[0:1, 0:128]),
                     reads=[("ps", "band")], writes=[("phrr", slot)])
                p.dma("qa", phr_d[slot][:], phrr[slot][:],
                      reads=[("phrr", slot)], writes=[("phrd", slot)])
                p.dma("qa", phrc[slot][:],
                      bass.AP(phr_d[slot].tensor, 0, [[1, 128], [1, 1]]),
                      reads=[("phrd", slot)], writes=[("phrc", slot)])

            def l1_stageB(gt):
                sq, t = divmod(gt, NS)
                slot = gt % 2
                kcur, kp1, kp2 = gt % 4, (gt - 1) % 4, (gt - 2) % 4
                pcur, pprev = gt % 3, (gt - 1) % 3
                skewsrc = bass.AP(sband_d[slot].tensor, 0, [[385, 128], [1, LB]])
                p.dma("qs", s2r[:], skewsrc,
                      reads=[("sbandd", slot)], writes=[("s2r",)])
                if t < 2:
                    p.op("v", lambda v, t=t: v.tensor_tensor(
                            s2r[:], s2r[:], vm_sb[t][:], ALU.mult),
                         reads=[("s2r",), ("w", f"vm{t}")], writes=[("s2r",)])
                    p.op("v", lambda v, t=t: v.tensor_tensor(
                            s2r[:], s2r[:], vmn_sb[t][:], ALU.add),
                         reads=[("s2r",), ("w", f"vmn{t}")], writes=[("s2r",)])
                p.op("v", lambda v: v.reduce_max(st2[:, 0:1], s2r[:], axis=AX.X),
                     reads=[("s2r",)], writes=[("st2",)])
                p.op("v", lambda v: v.tensor_copy(mr8[:, 0:1], st2[:, 0:1]),
                     reads=[("st2",)], writes=[("mr8",)])
                p.op("v", lambda v: v.match_replace(wrk[:], mr8[:], s2r[:],
                                                    NEG),
                     reads=[("s2r",), ("mr8",)], writes=[("wrk",)])
                for it in range(1, TK - 1):
                    p.op("v", lambda v: v.reduce_max(mr8[:, 0:1], wrk[:],
                                                     axis=AX.X),
                         reads=[("wrk",)], writes=[("mr8",)])
                    p.op("v", lambda v: v.match_replace(wrk[:], mr8[:],
                                                        wrk[:], NEG),
                         reads=[("wrk",), ("mr8",)], writes=[("wrk",)])
                p.op("v", lambda v: v.reduce_max(st2[:, 1:2], wrk[:], axis=AX.X),
                     reads=[("wrk",)], writes=[("st2",)])
                p.op("v", lambda v: v.tensor_scalar(
                        st2[:, 2:3], phrc[slot][:], 1.0 / 32.0, None, ALU.mult),
                     reads=[("phrc", slot)], writes=[("st2",)])
                p.op("v", lambda v: v.tensor_tensor(
                        st2[:, 3:4], st2[:, 0:1], phrc[slot][:], ALU.max),
                     reads=[("st2",), ("phrc", slot)], writes=[("st2",)])
                p.op("v", lambda v: v.tensor_scalar(
                        st2[:, 4:5], st2[:, 3:4], -1.0 / 32.0, None, ALU.mult),
                     reads=[("st2",)], writes=[("st2",)])
                p.op("s", lambda s: s.activation(expv[:], s2r[:], AF.Exp,
                                                 bias=st2[:, 4:5],
                                                 scale=1.0 / 32.0),
                     reads=[("s2r",), ("st2",)], writes=[("expv",)])
                p.op("v", lambda v: v.tensor_scalar(em[:], s2r[:], st2[:, 1:2],
                                                    None, ALU.is_ge),
                     reads=[("s2r",), ("st2",)], writes=[("em",)])
                p.op("v", lambda v: v.tensor_tensor(em[:], em[:], expv[:],
                                                    ALU.mult),
                     reads=[("em",), ("expv",)], writes=[("em",)])
                p.op("v", lambda v: v.reduce_sum(st2[:, 5:6], em[:], axis=AX.X),
                     reads=[("em",)], writes=[("st2",)])
                p.op("s", lambda s: s.activation(st2[:, 6:7], phrc[slot][:], AF.Exp,
                                                 bias=st2[:, 4:5],
                                                 scale=1.0 / 32.0),
                     reads=[("phrc", slot), ("st2",)], writes=[("st2",)])
                p.op("v", lambda v: v.tensor_tensor(
                        st2[:, 5:6], st2[:, 5:6], expv[:, LB - 1:LB], ALU.add),
                     reads=[("st2",), ("expv",)], writes=[("st2",)])
                p.op("v", lambda v: v.tensor_tensor(
                        st2[:, 5:6], st2[:, 5:6], st2[:, 6:7], ALU.add),
                     reads=[("st2",)], writes=[("st2",)])
                p.op("v", lambda v: v.reciprocal(st2[:, 7:8], st2[:, 5:6]),
                     reads=[("st2",)], writes=[("st2",)])
                p.op("v", lambda v: v.tensor_scalar(w2r[:], em[:], st2[:, 7:8],
                                                    None, ALU.mult),
                     reads=[("em",), ("st2",)], writes=[("w2r",)])
                p.op("v", lambda v: v.scalar_tensor_tensor(
                        w2r[:, LB - 1:LB], expv[:, LB - 1:LB], st2[:, 7:8],
                        w2r[:, LB - 1:LB], ALU.mult, ALU.add),
                     reads=[("expv",), ("st2",), ("w2r",)], writes=[("w2r",)])
                p.op("v", lambda v: v.tensor_tensor(st2[:, 6:7], st2[:, 6:7],
                                                    st2[:, 7:8], ALU.mult),
                     reads=[("st2",)], writes=[("st2",)])
                p.dma("qa", w2r_d[slot][:, 128:128 + LB], w2r[:],
                      reads=[("w2r",)], writes=[("w2rd", slot)])
                wsksrc = bass.AP(w2r_d[slot].tensor, 128, [[511, 128], [1, 384]])
                p.dma("qs", wsk[:], wsksrc,
                      reads=[("w2rd", slot)], writes=[("wsk",)])
                p.op("t", lambda t_: [
                        t_.transpose(ps_wt[:, w * 128:(w + 1) * 128],
                                     wsk[:, w * 128:(w + 1) * 128],
                                     c.ident_b[:]) for w in range(3)][-1],
                     reads=[("wsk",), ("c", "idb")], writes=[("ps", "wt")])
                p.op("v", lambda v: v.tensor_copy(
                        wT_sb[:].rearrange("p a b -> p (a b)"), ps_wt[:]),
                     reads=[("ps", "wt")], writes=[("wTsb",)])
                def mkwm(kcur=kcur, kp1=kp1, kp2=kp2):
                    ring_idx = [kp2, kp1, kcur]

                    def f(t_):
                        last = None
                        for nch in range(2):
                            for w in range(3):
                                last = t_.matmul(
                                    ps_o[:, nch * 512:(nch + 1) * 512],
                                    wT_sb[:, w, :],
                                    v_r[ring_idx[w]][:, nch * 512:(nch + 1) * 512],
                                    start=(w == 0), stop=(w == 2))
                        return last
                    return f
                p.op("t", mkwm(),
                     reads=[("wTsb",), ("vr", kcur), ("vr", kp1), ("vr", kp2)],
                     writes=[("ps", "o")])
                p.dma("qg", pvprev[1:128, :], pv_r[pcur][0:127, :],
                      reads=[("pvr", pcur)], writes=[("pvprev",)])
                if t == 0:
                    p.op("v", lambda v: v.memset(pvprev[0:1, :], 0.0),
                         writes=[("pvprev",)])
                else:
                    p.dma("qg", pvprev[0:1, :], pv_r[pprev][127:128, :],
                          reads=[("pvr", pprev)], writes=[("pvprev",)])
                p.op("v", lambda v: v.scalar_tensor_tensor(
                        out_sb[:], pvprev[:], st2[:, 6:7], ps_o[:],
                        ALU.mult, ALU.add),
                     reads=[("pvprev",), ("st2",), ("ps", "o")],
                     writes=[("outsb",)])
                if t == 0:
                    p.op("v", lambda v: v.memset(out_sb[0:1, :], 0.0),
                         writes=[("outsb",)])
                p.op("v", lambda v, sl=slot: v.tensor_tensor(
                        c.xn_sb[sl][:], out_sb[:], c.x_sb[sl][:], ALU.add),
                     reads=[("outsb",), ("xsb", slot)], writes=[("xsb", slot)])
                p.dma("qa", xout[gt * 128:(gt + 1) * 128, :], c.xn_sb[slot][:],
                      reads=[("xsb", slot)], writes=[(xout_tag, gt)])

            NTILE = cfg.Bc * NS
            l1_stageA(0)
            for gt in range(1, NTILE):
                l1_stageA(gt)
                l1_stageB(gt - 1)
            l1_stageB(NTILE - 1)
            p.full_barrier()


def emit_bf16_conversion(p, stk, nc, names, conv_list):
    with ExitStack() as cv:
        cf = [cv.enter_context(nc.sbuf_tensor(uname(f"cvf{i}"), [128, 4096], F32))
              for i in range(2)]
        cb = [cv.enter_context(nc.sbuf_tensor(uname(f"cvb{i}"), [128, 4096], BF16))
              for i in range(2)]
        for nm in conv_list:
            srcap = names[nm]
            shape = list(srcap.shape)
            nelem = int(np.prod(shape))
            dstt = nc.dram_tensor(uname(nm + "B"), shape, BF16, kind="Internal")
            dstap = dstt.ap()
            off = 0
            i = 0
            while off < nelem:
                w = min(4096, (nelem - off) // 128)
                assert w > 0 and (nelem - off) % 128 == 0
                sl = i % 2
                sap = bass.AP(srcap.tensor, off, [[w, 128], [1, w]])
                dap = bass.AP(dstap.tensor, off, [[w, 128], [1, w]])
                p.dma("qs", cf[sl][:, 0:w], sap, writes=[("cvf", sl)])
                eng = "v" if i % 2 == 0 else "s"
                if eng == "v":
                    p.op("v", lambda v, sl=sl, w=w: v.tensor_copy(
                            cb[sl][:, 0:w], cf[sl][:, 0:w]),
                         reads=[("cvf", sl)], writes=[("cvb", sl)])
                else:
                    p.op("s", lambda s, sl=sl, w=w: s.copy(
                            cb[sl][:, 0:w], cf[sl][:, 0:w]),
                         reads=[("cvf", sl)], writes=[("cvb", sl)])
                p.dma("qa", dap, cb[sl][:, 0:w], reads=[("cvb", sl)],
                      writes=[("cvout",)])
                off += 128 * w
                i += 1
            names[nm] = dstap
    p.full_barrier()


PHASES = {}


def _register_phases():
    PHASES["moe"] = emit_phase_moe
    PHASES["l1ffn"] = emit_phase_l1ffn
    try:
        PHASES["l0attn"] = emit_phase_l0attn
        PHASES["l1attn"] = emit_phase_l1attn
    except NameError:
        pass


def build_program(cfg, phase_names, input_specs):
    nc = bass.Bass("TRN2", target_bir_lowering=False, debug=False)
    names = {}
    for nm, (shape, dt) in input_specs.items():
        names[nm] = nc.dram_tensor(nm, list(shape), dt, kind="ExternalInput").ap()
    xin = nc.dram_tensor("xin", [cfg.T, cfg.D], F32, kind="ExternalInput").ap()
    xout = nc.dram_tensor("xout", [cfg.T, cfg.D], F32, kind="ExternalOutput").ap()
    xa = nc.dram_tensor("xa", [cfg.T, cfg.D], F32, kind="Internal").ap()
    xb = nc.dram_tensor("xb", [cfg.T, cfg.D], F32, kind="Internal").ap()
    nph = len(phase_names)
    chain = []
    cur = (xin, "XIN")
    for i, pn in enumerate(phase_names):
        if i == nph - 1:
            nxt = (xout, "XOUT")
        elif i % 2 == 0:
            nxt = (xa, "XA")
        else:
            nxt = (xb, "XB")
        chain.append((pn, cur, nxt))
        cur = nxt

    with ExitStack() as stk:
        sems = {}
        for k in ("t", "v", "s", "g"):
            sems[k] = stk.enter_context(nc.semaphore(f"sem_{k}"))
        for q in ("qs", "qa", "qg"):
            for i in range(Prog.NSUB):
                sems[(q, i)] = stk.enter_context(nc.semaphore(f"sem_{q}{i}"))
        block = stk.enter_context(nc.Block())
        p = Prog(nc, block, sems)
        conv_list = ["fc", "pjT", "wq", "wk", "wv", "wo",
                     "qwT", "kwT", "vwT", "pkwT", "pvwT",
                     "win", "wouts", "extk", "exrtk"]
        emit_bf16_conversion(p, stk, nc, names, conv_list)
        c = alloc_common(stk, p, nc, cfg, names)
        c.nc = nc
        c.cfg = cfg
        for pn, (srcx, stag), (dst, dtag) in chain:
            PHASES[pn](p, stk, nc, c, cfg, srcx, dst, stag, dtag)
            p.full_barrier()
        p.drain()
    return nc


def _stack2(a):
    return np.concatenate([a, a], axis=0)


def prep_weight_inputs(cfg, I):
    P = {}
    P["rwT"] = np.ascontiguousarray(I["L0_ffn_router_w"].T)
    P["fc"] = np.ascontiguousarray(I["L0_ffn_expert_fc"])
    P["pjT"] = np.ascontiguousarray(I["L0_ffn_expert_proj"].transpose(0, 2, 1))
    P["wq"] = np.ascontiguousarray(I["L0_attn_Wq"])
    P["wk"] = np.ascontiguousarray(I["L0_attn_Wk"])
    P["wv"] = np.ascontiguousarray(I["L0_attn_Wv"])
    P["wo"] = np.ascontiguousarray(I["L0_attn_Wo"])
    P["bqp"] = np.ascontiguousarray(I["L0_attn_bq"].reshape(8, 128).T)
    P["bkp"] = np.ascontiguousarray(I["L0_attn_bk"].reshape(8, 128).T)
    P["bvp"] = np.ascontiguousarray(I["L0_attn_bv"].reshape(8, 128).T)
    P["bo"] = I["L0_attn_bo"].reshape(1, 1024)
    P["lng"] = I["L0_attn_ln_gamma"].reshape(1, 1024)
    P["lnb"] = I["L0_attn_ln_beta"].reshape(1, 1024)
    S, M = cfg.S, cfg.M
    de = I["L0_attn_dist_emb"]
    jp = np.arange(cfg.Wp)
    idx = np.clip(jp - (S - 1) + (M - 1), 0, 2 * M - 2)
    ext = np.ascontiguousarray(de[idx].T)
    idx_r = np.clip((2 * S - 2) - jp - (S - 1) + (M - 1), 0, 2 * M - 2)
    exrt = np.ascontiguousarray(de[idx_r].T)
    P["extk"] = np.ascontiguousarray(np.concatenate([ext, ext], axis=0))
    P["exrtk"] = np.ascontiguousarray(np.concatenate([exrt, exrt], axis=0))
    for nm in ("q", "k", "v", "pk", "pv"):
        P[nm + "wT"] = np.ascontiguousarray(I[f"L1_attn_{nm}_w"].T)
    P["qbp"] = np.ascontiguousarray(I["L1_attn_q_b"].reshape(8, 128).T)
    P["kbp"] = np.ascontiguousarray(I["L1_attn_k_b"].reshape(8, 128).T)
    P["pkbp"] = np.ascontiguousarray(I["L1_attn_pk_b"].reshape(8, 128).T)
    P["vbr"] = I["L1_attn_v_b"].reshape(1, 1024)
    P["pvbr"] = I["L1_attn_pv_b"].reshape(1, 1024)
    P["win"] = np.ascontiguousarray(I["L1_ffn_Win"])
    P["bins"] = _stack2(I["L1_ffn_bin_"]).reshape(128, 1)
    P["w1s"] = np.ascontiguousarray(_stack2(I["L1_ffn_W1"]))
    P["b1s"] = np.ascontiguousarray(_stack2(I["L1_ffn_b1"]))
    P["w2s"] = np.ascontiguousarray(_stack2(I["L1_ffn_W2"]))
    P["b2s"] = _stack2(I["L1_ffn_b2"]).reshape(128, 1)
    P["wouts"] = np.ascontiguousarray(_stack2(I["L1_ffn_Wout"]))
    P["bout"] = I["L1_ffn_bout"].reshape(1, 1024)
    P["ident_f"] = np.eye(128, dtype=np.float32)
    P["ident_b"] = np.eye(128, dtype=BFDT)
    lm = np.zeros((128, 128), dtype=np.float32)
    for blk in range(128 // cfg.CH):
        for i in range(cfg.CH):
            for i2 in range(i + 1):
                lm[blk * cfg.CH + i2, blk * cfg.CH + i] = 1.0 / (i + 1)
    P["lmatT"] = lm.astype(BFDT)
    sr = np.arange(128)[:, None]
    dp = np.arange(cfg.LB)[None, :]
    P["vm0"] = ((sr + dp) >= cfg.LB).astype(np.float32)
    P["vm1"] = ((sr + dp) >= cfg.LB - 128).astype(np.float32)
    P["vmn0"] = (NEG * (1.0 - P["vm0"])).astype(np.float32)
    P["vmn1"] = (NEG * (1.0 - P["vm1"])).astype(np.float32)
    P["ones_row"] = np.ones((1, 128), dtype=BFDT)
    return P


def input_specs_of(P):
    out = {}
    for k, v in P.items():
        dt = {np.dtype(np.float32): F32, np.dtype(BFDT): BF16}[v.dtype]
        out[k] = (v.shape, dt)
    return out


def kernel(**inputs):
    _register_phases()
    cfg = CFG(S=2048, M=1024)
    x = np.ascontiguousarray(inputs["L0_attn_hidden_states"], dtype=np.float32)
    B, S, D = x.shape
    assert (B, S, D) == (16, 2048, 1024)
    P = prep_weight_inputs(cfg, inputs)
    specs = input_specs_of(P)
    phase_seq = ["moe", "l0attn", "moe", "l1ffn", "l1attn", "l1ffn"]
    fuse = os.environ.get("ATHENA_UNFUSED", "0") != "1"
    shards = [np.ascontiguousarray(x[2 * i:2 * i + 2].reshape(cfg.T, cfg.D))
              for i in range(8)]
    if fuse:
        nc = build_program(cfg, phase_seq, specs)
        in_maps = [dict(P, xin=shards[i]) for i in range(8)]
        res = run_bass_kernel_spmd(nc, in_maps, core_ids=list(range(8)))
        outs = [r["xout"] for r in res.results]
    else:
        cur = shards
        for pn in phase_seq:
            nc = build_program(cfg, [pn], specs)
            in_maps = [dict(P, xin=cur[i]) for i in range(8)]
            res = run_bass_kernel_spmd(nc, in_maps, core_ids=list(range(8)))
            cur = [np.asarray(r["xout"], dtype=np.float32) for r in res.results]
        outs = cur
    return np.concatenate([o.reshape(2, S, D) for o in outs], axis=0)
